# version 43
# baseline (speedup 1.0000x reference)
import numpy as np
import concourse.bass as bass
import concourse.mybir as mybir
from concourse.bass_utils import run_bass_kernel_spmd

F32 = mybir.dt.float32
BF16 = mybir.dt.bfloat16
AF = mybir.ActivationFunctionType
ALU = mybir.AluOpType

NCORES = 8
D, KC, NST = 1024, 8, 16
TW = 2048
ATILE = 512
NAT = TW // ATILE
DTILE = 256
NDT = TW // DTILE
F, FC = 4096, 32
EPS_LN = 1e-5
EPS_RMS = 1e-6

XW = 0
WG = 16384
XPB = 24576
ZB = 28672
LAMC = 32768
BXC = 34816
HSC = 36864
ST = 38912
BC1 = 41472
GT1 = 45568
GT2 = 46080
OB = 46592
RW = 47104
SUM4C = 49152
NUM4C = 49664
ACOLS = 50176

ST_WLB = ST
ST_WR = ST + 264
ST_WS = ST + 528
ST_WSQ = ST + 792
ST_WC = ST + 1056
ST_ZO = ST + 2080
ST_ON = ST + 2113
ST_B1 = ST + 2241
ST_B2 = ST + 2273
ST_RWT = ST + 2281
ST_BL = ST + 2289
ST_WSL = ST + 2290
ST_WSR = ST + 2291
ST_WGS = ST + 2292
ST_CAR = ST + 2300
ST_ZSEL = ST + 2303
ST_EP5 = ST + 2301
ST_EP6 = ST + 2302

ENGINES = ("sync", "gpsimd", "scalar", "vector", "tensor")


class Sched:
    def __init__(self):
        self.ops = {e: [] for e in ENGINES}
        self.cnt = {}

    def emit(self, eng, fn, waits=(), inc=None):
        tok = None
        if inc is not None:
            s, a = inc
            self.cnt[s] = self.cnt.get(s, 0) + a
            tok = (s, self.cnt[s])
        self.ops[eng].append((fn, tuple(w for w in waits if w is not None), inc))
        return tok


def build_nc(dbg=(), skip_d=False):
    nc = bass.Bass("TRN2", target_bir_lowering=False,
                   detect_race_conditions=False)

    xw_h = nc.dram_tensor("xw", [128, KC * TW], F32, kind="ExternalInput").ap()
    xp_h = nc.dram_tensor("xp", [128, KC * TW], F32, kind="ExternalInput").ap()
    wlb_h = nc.dram_tensor("wlb", [128, KC * 33], F32, kind="ExternalInput").ap()
    wr_h = nc.dram_tensor("wr", [128, KC * 33], F32, kind="ExternalInput").ap()
    ws_h = nc.dram_tensor("ws33", [128, 8 * 33], F32, kind="ExternalInput").ap()
    wg_h = nc.dram_tensor("wg", [128, 8192], F32, kind="ExternalInput").ap()
    wc_h = nc.dram_tensor("wc", [16, 1024], F32, kind="ExternalInput").ap()
    w1_h = nc.dram_tensor("w1", [FC * 128, 1024], F32, kind="ExternalInput").ap()
    w2_h = nc.dram_tensor("w2", [FC * 128, 1024], F32, kind="ExternalInput").ap()
    zo_h = nc.dram_tensor("zo33", [128, 33], F32, kind="ExternalInput").ap()
    zs_h = nc.dram_tensor("zsel", [128, 33], F32, kind="ExternalInput").ap()
    on_h = nc.dram_tensor("ones", [128, 128], F32, kind="ExternalInput").ap()
    b1_h = nc.dram_tensor("b1c", [128, 32], F32, kind="ExternalInput").ap()
    b2_h = nc.dram_tensor("b2c", [128, 8], F32, kind="ExternalInput").ap()
    rw_h = nc.dram_tensor("rmsw", [128, 8], F32, kind="ExternalInput").ap()
    bl_h = nc.dram_tensor("b_lam", [16, 1], F32, kind="ExternalInput").ap()
    out_h = nc.dram_tensor("out", [128, KC * TW], F32, kind="ExternalOutput").ap()
    w1s_h = nc.dram_tensor("w1s", [FC * 128, 2048], BF16, kind="Internal").ap()
    dbg_h = {}
    for name, shape in dbg:
        dbg_h[name] = nc.dram_tensor("dbg_" + name, shape, F32, kind="ExternalOutput").ap()

    Ar = nc.alloc_sbuf_tensor("arena", [128, ACOLS], F32)
    pb = [nc.alloc_psum_tensor(f"pb{i}", [128, 512], F32) for i in range(8)]
    pA, pB, pC, pD, pS, pS2, pBC, pBC2 = pb

    S = Sched()
    T, V, SC, Q, G = "T", "V", "SC", "Q", "G"

    def dma(q, sem, out, in_, waits=()):
        return S.emit(q, lambda e, out=out, in_=in_: e.dma_start(out=out, in_=in_),
                      waits, (sem, 16))

    def xw_sl(kc, c0, n):
        return Ar[:, XW + kc * TW + c0: XW + kc * TW + c0 + n]

    ones_col = Ar[:, ST_ON: ST_ON + 1]

    def row(i, n=512):
        return Ar[32:33, RW + i * 512: RW + i * 512 + n]

    tk_c = {}
    tk_c["wlb"] = dma("sync", "QC", Ar[:, ST_WLB: ST_WLB + 264], wlb_h)
    tk_c["wr"] = dma("sync", "QC", Ar[:, ST_WR: ST_WR + 264], wr_h)
    tk_c["ws"] = dma("sync", "QC", Ar[:, ST_WS: ST_WS + 264], ws_h)
    tk_c["wc"] = dma("sync", "QC", Ar[0:16, ST_WC: ST_WC + 1024], wc_h)
    tk_c["zo"] = dma("sync", "QC", Ar[:, ST_ZO: ST_ZO + 33], zo_h)
    tk_c["zs"] = dma("sync", "QC", Ar[:, ST_ZSEL: ST_ZSEL + 33], zs_h)
    tk_c["on"] = dma("sync", "QC", Ar[:, ST_ON: ST_ON + 128], on_h)
    tk_c["b1"] = dma("sync", "QC", Ar[:, ST_B1: ST_B1 + 32], b1_h)
    tk_c["b2"] = dma("sync", "QC", Ar[:, ST_B2: ST_B2 + 8], b2_h)
    tk_c["rw"] = dma("sync", "QC", Ar[:, ST_RWT: ST_RWT + 8], rw_h)
    tk_c["bl"] = dma("sync", "QC", Ar[0:16, ST_BL: ST_BL + 1], bl_h)
    _qc_all = ("QC", S.cnt["QC"])
    for _k in tk_c:
        tk_c[_k] = _qc_all
    tk_c["xw"] = ("QX", 8 * 16)
    tk_c["wg"] = ("QG", 4 * 16)

    tk_ep5 = S.emit("vector", lambda e: e.memset(
        Ar[:, ST_EP5: ST_EP5 + 1], EPS_LN), (), (V, 1))
    tk_ep6 = S.emit("vector", lambda e: e.memset(
        Ar[:, ST_EP6: ST_EP6 + 1], EPS_RMS), (), (V, 1))
    zero_b = Ar[:, SUM4C: SUM4C + 256]
    tk_z0 = S.emit("vector", lambda e: e.memset(zero_b, 0.0), (), (V, 1))
    h_b = Ar[:, BXC + 1024: BXC + 1280]
    P_b = Ar[:, BXC + 1280: BXC + 1536]
    hl8 = Ar[0:16, BXC + 512: BXC + 520]
    pl8 = Ar[0:16, BXC + 520: BXC + 528]
    compb = Ar[0:16, BXC + 528: BXC + 536]
    carr_bc = Ar[:, BXC + 544: BXC + 545]

    def ep5():
        return Ar[32:33, ST_EP5: ST_EP5 + 1]

    def ep6():
        return Ar[32:33, ST_EP6: ST_EP6 + 1]

    tk_wsq = S.emit("scalar", lambda e: e.activation(
        Ar[:, ST_WSQ: ST_WSQ + 264], Ar[:, ST_WS: ST_WS + 264], AF.Square),
        (tk_c["ws"],), (SC, 1))

    for kc in range(KC):
        S.emit("tensor", lambda e, kc=kc: e.matmul(
            pS[0:33, 0:1], Ar[:, ST_WLB + kc * 33: ST_WLB + kc * 33 + 33],
            ones_col, start=(kc == 0), stop=(kc == KC - 1)),
            (tk_c["wlb"], tk_c["on"]),
            (T, 1) if kc == KC - 1 else None)
    tk_mm = (T, S.cnt[T])
    tk_wsl = S.emit("vector", lambda e: e.tensor_copy(
        Ar[0:33, ST_WSL: ST_WSL + 1], pS[0:33, 0:1]), (tk_mm,), (V, 1))
    for kc in range(KC):
        S.emit("tensor", lambda e, kc=kc: e.matmul(
            pS2[0:33, 0:1], Ar[:, ST_WR + kc * 33: ST_WR + kc * 33 + 33],
            ones_col, start=(kc == 0), stop=(kc == KC - 1)),
            (tk_c["wr"],), (T, 1) if kc == KC - 1 else None)
    tk_mm = (T, S.cnt[T])
    tk_wsr = S.emit("vector", lambda e: e.tensor_copy(
        Ar[0:33, ST_WSR: ST_WSR + 1], pS2[0:33, 0:1]), (tk_mm,), (V, 1))
    tk_wgs = None

    bfree = {"pS": tk_wsl, "pS2": tk_wsr, "pBC": None, "pBC2": None,
             "pA": None, "pB": None, "pC": None, "pD": None}
    tmp32_free = [(), ()]
    sq_free = [None, None]
    xpb_free = None
    lam_toks = []
    lamtmp_free = [None]
    tk_scan1 = None
    TI = {}
    prev_bc = {"r": None, "n": None, "nmr": None, "ln": None}
    sp_cons = [(), (), (), ()]
    sp_wout = [None, None, None, None]
    sp_wb_pend = []

    def emit_split_wb(t, s, tk_lo):
        dst_h = w1s_h
        fc = (t % 64) // 2
        h = t % 2
        hilo = Ar[:, ZB + s * 1024 + 512: ZB + s * 1024 + 1024].bitcast(BF16)
        sp_wout[s] = dma("sync", f"GW{s}",
                         dst_h[fc * 128: fc * 128 + 128,
                               h * 1024: h * 1024 + 1024],
                         hilo, (tk_lo,))

    xp_tok = {}

    def a_head(a):
        prefix = a < 4
        tt = a if prefix else a - 4
        even = (a % 2 == 0)
        pst, pst2 = (pS, pS2) if even else (pC, pD)
        pbr, pbn = (pA, pB) if even else (pBC, pBC2)
        nst, nst2 = ("pS", "pS2") if even else ("pC", "pD")
        nbr, nbn = ("pA", "pB") if even else ("pBC", "pBC2")

        if prefix:
            if a == 0:
                xp_tok[0] = dma("scalar", "QP", Ar[:, XPB: XPB + 4096],
                                xp_h[:, 0: 4096], ())
            tk_src = xp_tok[a]
            def src(kc, a=a):
                base = XPB if a % 2 == 0 else BC1
                return Ar[:, base + kc * 512: base + kc * 512 + 512]
        else:
            tk_src = tk_c["xw"]
            def src(kc, tt=tt):
                return xw_sl(kc, tt * ATILE, 512)

        for kc in range(KC):
            S.emit("tensor", lambda e, kc=kc, src=src, pst=pst: e.matmul(
                pst[0:33, :], Ar[:, ST_WLB + kc * 33: ST_WLB + kc * 33 + 33],
                src(kc), start=(kc == 0), stop=(kc == KC - 1)),
                (tk_src, tk_c["wlb"], bfree[nst]) if kc == 0 else (),
                (T, 1) if kc == KC - 1 else None)
        tk_lb = (T, S.cnt[T])
        for kc in range(KC):
            par = kc % 2
            sqsl = Ar[:, (OB if par == 0 else GT1): (OB if par == 0 else GT1) + 512]
            tk_sq_w = S.emit("vector", lambda e, sqsl=sqsl, kc=kc, src=src:
                             e.tensor_mul(sqsl, src(kc), src(kc)),
                             (tk_src, sq_free[par]), (V, 1))
            tk_pass = S.emit("tensor", lambda e, sqsl=sqsl, kc=kc, pst2=pst2: e.matmul(
                pst2[0:33, :], Ar[:, ST_ZO: ST_ZO + 33], sqsl,
                start=(kc == 0), stop=(kc == KC - 1)),
                (tk_sq_w, tk_c["zo"]) + ((bfree[nst2],) if kc == 0 else ()),
                (T, 1))
            sq_free[par] = tk_pass
        tk_sq = (T, S.cnt[T])
        TI[a] = dict(src=src, tk_lb=tk_lb, tk_sq=tk_sq, pst=pst, pst2=pst2,
                     pbr=pbr, pbn=pbn, nst=nst, nst2=nst2, nbr=nbr, nbn=nbn,
                     tt=tt, prefix=prefix)
        if prefix and a + 1 < 4:
            nb = XPB if (a + 1) % 2 == 0 else BC1
            w = (TI[a - 1]["tk_sq"],) if a >= 1 else ()
            xp_tok[a + 1] = dma("scalar", "QP", Ar[:, nb: nb + 4096],
                                xp_h[:, (a + 1) * 4096: (a + 1) * 4096 + 4096],
                                w)

    for a in range(8):
        if a != 5:
            a_head(a)
        for kc_ in {0: (0, 1), 1: (2, 3, 4), 2: (5, 6, 7)}.get(a, ()):
            dma("sync", "QX", Ar[:, XW + kc_ * TW: XW + kc_ * TW + TW],
                xw_h[:, kc_ * TW: kc_ * TW + TW])
        for q_ in {2: (0, 1), 3: (2, 3)}.get(a, ()):
            dma("sync", "QG", Ar[:, WG + q_ * 2048: WG + q_ * 2048 + 2048],
                wg_h[:, q_ * 2048: q_ * 2048 + 2048])

        def a_tail(j, do_evac=True):
            ti = TI[j]
            pst, pst2, pbr, pbn = ti["pst"], ti["pst2"], ti["pbr"], ti["pbn"]
            tt, prefix = ti["tt"], ti["prefix"]
            S.emit("vector", lambda e, pst=pst: e.tensor_single_scalar(
                row(0), pst[32:33, :], -1.0 / D, ALU.mult),
                (ti["tk_lb"], prev_bc["n"]), None)
            S.emit("vector", lambda e, pst2=pst2: e.tensor_single_scalar(
                row(1), pst2[32:33, :], 1.0 / D, ALU.mult),
                (ti["tk_sq"], prev_bc["ln"]), None)
            S.emit("vector", lambda e: e.tensor_mul(row(2), row(0), row(0)), (), None)
            S.emit("vector", lambda e: e.tensor_sub(row(1), row(1), row(2)), (), (V, 1))
            tk_var = (V, S.cnt[V])
            S.emit("scalar", lambda e: e.activation(
                row(2), row(1), AF.Ln, bias=ep5()), (tk_var, tk_ep5), (SC, 1))
            prev_bc["ln"] = (SC, S.cnt[SC])
            S.emit("scalar", lambda e: e.activation(
                row(3), row(2), AF.Exp, scale=-0.5),
                (prev_bc["ln"], prev_bc["r"], prev_bc["nmr"]), (SC, 1))
            tk_rstd = (SC, S.cnt[SC])
            ti["tk_rstd"] = tk_rstd
            S.emit("vector", lambda e: e.tensor_mul(row(0), row(0), row(3)),
                   (tk_rstd,), (V, 1))
            tk_nmr = (V, S.cnt[V])
            prev_bc["nmr"] = tk_nmr
            S.emit("tensor", lambda e, pbr=pbr: e.matmul(
                pbr[:, :], Ar[32:33, ST_ON: ST_ON + 128], row(3),
                start=True, stop=True), (tk_rstd, bfree[ti["nbr"]]), (T, 1))
            tk_bcr = (T, S.cnt[T])
            S.emit("tensor", lambda e, pbn=pbn: e.matmul(
                pbn[:, :], Ar[32:33, ST_ON: ST_ON + 128], row(0),
                start=True, stop=True), (tk_nmr, bfree[ti["nbn"]]), (T, 1))
            tk_bcn = (T, S.cnt[T])
            prev_bc["r"], prev_bc["n"] = tk_bcr, tk_bcn
            if not prefix:
                S.emit("vector", lambda e, tt=tt, pbr=pbr: e.tensor_copy(
                    Ar[:, BC1 + tt * 512: BC1 + tt * 512 + 512], pbr[:, :]),
                    (tk_bcr, TI[3]["tk_sq"]), (V, 1))
                S.emit("vector", lambda e, tt=tt, pbn=pbn: e.tensor_copy(
                    Ar[:, BC1 + 2048 + tt * 512: BC1 + 2048 + tt * 512 + 512],
                    pbn[:, :]), (tk_bcn,), (V, 1))
            par = j % 2
            t32 = Ar[0:32, RW + par * 512: RW + par * 512 + 512]
            S.emit("vector", lambda e, t32=t32, pst=pst: e.tensor_copy(
                t32, pst[0:32, :]), (ti["tk_lb"],) + tuple(tmp32_free[par]), (V, 1))
            tk_corr0 = (V, S.cnt[V])
            S.emit("vector", lambda e, t32=t32, pbr=pbr: e.tensor_mul(
                t32, t32, pbr[0:32, :]), (tk_bcr,), (V, 1))
            tk_corr1 = (V, S.cnt[V])
            S.emit("vector", lambda e, t32=t32, pbn=pbn: e.scalar_tensor_tensor(
                t32, pbn[0:32, :], Ar[0:32, ST_WSL: ST_WSL + 1], t32,
                ALU.mult, ALU.add), (tk_bcn, tk_wsl), (V, 1))
            tk_corr2 = (V, S.cnt[V])
            bfree[ti["nst"]] = tk_corr0
            bfree[ti["nst2"]] = tk_corr1
            bfree[ti["nbr"]] = tk_corr1
            bfree[ti["nbn"]] = tk_corr2
            ti["t32"] = t32
            ti["tk_corr2"] = tk_corr2
            if do_evac:
                a_tail_evac(j)

        def a_tail_evac(j):
            ti = TI[j]
            tt, prefix = ti["tt"], ti["prefix"]
            t32, tk_corr2 = ti["t32"], ti["tk_corr2"]
            par = j % 2
            w_scan1 = tuple(tk_scan1) if (tk_scan1 is not None and not prefix) else ()
            lam_tmp = Ar[0:16, GT2: GT2 + 512]
            tk_sig = S.emit("scalar", lambda e, t32=t32, lam_tmp=lam_tmp: e.activation(
                lam_tmp, t32[0:16, :], AF.Sigmoid, bias=Ar[0:16, ST_BL: ST_BL + 1]),
                (tk_corr2, tk_c["bl"], lamtmp_free[0]) + w_scan1, (SC, 1))
            for h_ in (0, 1):
                c_ = 2 * tt + h_
                dma("scalar", "QLB", Ar[16 * c_: 16 * c_ + 16, LAMC: LAMC + 256],
                    lam_tmp[:, h_ * 256: h_ * 256 + 256], (tk_sig,))
                dma("scalar", "QLB", Ar[16 * c_: 16 * c_ + 16, BXC: BXC + 256],
                    t32[16:32, h_ * 256: h_ * 256 + 256], (tk_corr2,) + w_scan1)
            tk_lb = ("QLB", S.cnt["QLB"])
            lamtmp_free[0] = tk_lb
            lam_toks.append(tk_lb)
            tmp32_free[par] = (tk_sig, tk_lb)

        if a >= 1 and a != 5:
            a_tail(a - 1)
        if a == 4:
            a_tail(4, do_evac=False)
            a_head(5)
            lam_b = Ar[:, LAMC: LAMC + 256]
            bx_b = Ar[:, BXC: BXC + 256]
            tk_h1 = S.emit("vector", lambda e, lam_b=lam_b, bx_b=bx_b:
                           e.tensor_tensor_scan(h_b, lam_b, bx_b, 0.0,
                                                ALU.mult, ALU.add),
                           tuple(lam_toks), (V, 1))
            tk_p1 = S.emit("vector", lambda e, lam_b=lam_b:
                           e.tensor_tensor_scan(P_b, lam_b, zero_b, 1.0,
                                                ALU.mult, ALU.add),
                           (tk_z0,), (V, 1))
            tk_f1 = S.emit("scalar", lambda e: e.copy(
                Ar[:, BXC + 1264: BXC + 1280], Ar[:, BXC + 1264: BXC + 1280]),
                (tk_h1,), (SC, 1))
            tk_f2 = S.emit("scalar", lambda e: e.copy(
                Ar[:, BXC + 1520: BXC + 1536], Ar[:, BXC + 1520: BXC + 1536]),
                (tk_p1,), (SC, 1))
            for c_ in range(8):
                dma("scalar", "QG1", hl8[:, c_: c_ + 1],
                    Ar[16 * c_: 16 * c_ + 16, BXC + 1279: BXC + 1280], (tk_f1,))
                dma("scalar", "QG1", pl8[:, c_: c_ + 1],
                    Ar[16 * c_: 16 * c_ + 16, BXC + 1535: BXC + 1536], (tk_f2,))
            tk_cb1 = S.emit("vector", lambda e: e.tensor_tensor_scan(
                compb, pl8, hl8, 0.0, ALU.mult, ALU.add),
                (("QG1", S.cnt["QG1"]),), (V, 1))
            tk_scan1 = (S.emit("scalar", lambda e: e.copy(
                Ar[0:16, ST_CAR: ST_CAR + 1], compb[:, 7:8]),
                (tk_cb1,), (SC, 1)),)
            lam_toks = []
            a_tail_evac(4)
        if a == 4:
            for oc_ in range(8):
                for kc_ in range(KC):
                    S.emit("tensor", lambda e, kc=kc_, oc=oc_: e.matmul(
                        pBC[:, oc: oc + 1],
                        Ar[:, WG + (kc * 8 + oc) * 128:
                           WG + (kc * 8 + oc) * 128 + 128],
                        ones_col, start=(kc == 0 and oc == 0),
                        stop=(kc == KC - 1), skip_group_check=True),
                        (tk_c["wg"], bfree["pBC"]) if oc_ == 0 and kc_ == 0
                        else (),
                        (T, 1) if kc_ == KC - 1 else None)
            tk_wgs = S.emit("vector", lambda e: e.tensor_copy(
                Ar[:, ST_WGS: ST_WGS + 8], pBC[:, 0:8]),
                ((T, S.cnt[T]),), (V, 1))
            bfree["pBC"] = tk_wgs
        for t in range(a * 8, a * 8 + 8):
            s = t % 4
            src_h_ = w1_h
            fc = t // 2
            h = t % 2
            inb = Ar[:, ZB + s * 1024: ZB + s * 1024 + 512]
            hib = Ar[:, ZB + s * 1024 + 512: ZB + s * 1024 + 768].bitcast(BF16)
            lob = Ar[:, ZB + s * 1024 + 768: ZB + s * 1024 + 1024].bitcast(BF16)
            tk_in = dma("sync", f"QW{s}",
                        inb, src_h_[fc * 128: fc * 128 + 128,
                                    h * 512: h * 512 + 512],
                        sp_cons[s])
            tk_hi = S.emit("gpsimd", lambda e, hib=hib, inb=inb: e.tensor_copy(
                hib, inb), (tk_in, sp_wout[s]), ("GC", 1))
            tk_lo = S.emit("gpsimd", lambda e, lob=lob, inb=inb, hib=hib:
                           e.tensor_tensor(lob, inb, hib, ALU.subtract),
                           (tk_hi,), ("GC", 1))
            sp_cons[s] = (tk_hi, tk_lo)
            sp_wb_pend.append((t, s, tk_lo))
            if len(sp_wb_pend) > 2:
                emit_split_wb(*sp_wb_pend.pop(0))
    for args in sp_wb_pend:
        emit_split_wb(*args)
    sp_wb_pend = []
    a_tail(7)

    last_pS_reader = bfree["pS"]
    last_pS2_reader = bfree["pS2"]
    last_pBC_reader = bfree["pBC"]
    last_pBC2_reader = bfree["pBC2"]

    gbank = [pA, pB]
    ybank = [pC, pD]
    g_free = [bfree["pA"], bfree["pB"]]
    y_free = [bfree["pC"], bfree["pD"]]
    z_free = {}
    gt_state = {"gt1": [None, None],
                "gt2": [("QLB", S.cnt["QLB"]), None], "zsq": None,
                "pBC2": bfree["pBC2"], "prevbc": (TI[7]["tk_rstd"],)}
    stat_free = {"pS": bfree["pS"], "pS2": bfree["pS2"], "pBC": bfree["pBC"]}
    r1_tok = {}
    CT = {}

    def zsl_ap(tt, oc):
        base = ZB if tt % 2 == 0 else LAMC
        return Ar[:, base + oc * 512: base + oc * 512 + 512]

    def ab_ap(tt, which):
        c0 = XPB + (tt % 2) * 1024 + (0 if which == "a" else 512)
        return Ar[:, c0: c0 + 512]

    def c_head_gates(tt):
        st_g = {}
        tk_gate_last = None
        for oc in range(8):
            par = (tt * 8 + oc) % 2
            for kc in range(KC):
                S.emit("tensor", lambda e, kc=kc, oc=oc, tt=tt, par=par: e.matmul(
                    gbank[par][:, :],
                    Ar[:, WG + (kc * 8 + oc) * 128: WG + (kc * 8 + oc) * 128 + 128],
                    xw_sl(kc, tt * ATILE, 512),
                    start=(kc == 0), stop=(kc == KC - 1)),
                    (tk_c["wg"], tk_c["xw"], g_free[par]) if kc == 0 else (),
                    (T, 1) if kc == KC - 1 else None)
            tk_gmm = (T, S.cnt[T])
            tk_gate_last = tk_gmm
            bcr = Ar[:, BC1 + tt * 512: BC1 + tt * 512 + 512]
            bcn = Ar[:, BC1 + 2048 + tt * 512: BC1 + 2048 + tt * 512 + 512]
            g1b_ = Ar[:, GT1: GT1 + 512] if par == 0 \
                else Ar[:, XPB + 2048: XPB + 2048 + 512]
            S.emit("vector", lambda e, par=par, bcr=bcr, g1b_=g1b_: e.tensor_mul(
                g1b_, gbank[par][:, :], bcr),
                (tk_gmm, gt_state["gt1"][par]), (V, 1))
            tk_g1 = (V, S.cnt[V])
            g_free[par] = tk_g1
            S.emit("vector", lambda e, oc=oc, bcn=bcn, g1b_=g1b_: e.scalar_tensor_tensor(
                g1b_, bcn,
                Ar[:, ST_WGS + oc: ST_WGS + oc + 1], g1b_,
                ALU.mult, ALU.add), (tk_wgs,), (V, 1))
            tk_g1b = (V, S.cnt[V])
            w_split = tuple((f"GW{s_}", S.cnt.get(f"GW{s_}", 0))
                            for s_ in range(4)) if tt == 0 else ()
            w_scr = (tk_scan2,) if tt == 1 else ()
            S.emit("scalar", lambda e, g1b_=g1b_, tt=tt, oc=oc: e.activation(
                zsl_ap(tt, oc), g1b_, AF.Sigmoid),
                (tk_g1b, z_free.get((tt - 2, oc))) + w_split + w_scr, (SC, 1))
            tk_g2 = (SC, S.cnt[SC])
            gt_state["gt1"][par] = tk_g2
            st_g[oc] = (tk_g2, par)
        CT[tt] = dict(tk_gate_last=tk_gate_last, st_g=st_g)

    def c_head_wz(tt):
        ct = CT[tt]
        tk_g = {}
        for oc in range(8):
            tk_g2, par = ct["st_g"][oc]
            S.emit("tensor", lambda e, oc=oc, tt=tt, par=par: e.matmul(
                ybank[par][:, :],
                Ar[0:16, ST_WC + oc * 128: ST_WC + oc * 128 + 128],
                Ar[0:16, HSC + tt * 512: HSC + tt * 512 + 512],
                start=True, stop=True),
                (tk_scan2, tk_c["wc"], y_free[par]), (T, 1))
            tk_ymm = (T, S.cnt[T])
            S.emit("vector", lambda e, par=par, tt=tt, oc=oc: e.tensor_mul(
                zsl_ap(tt, oc), ybank[par][:, :], zsl_ap(tt, oc)),
                (tk_ymm, tk_g2), (V, 1))
            tk_z = (V, S.cnt[V])
            y_free[par] = tk_z
            gt_state["gt2"][par] = tk_z
            tk_g[oc] = tk_z
        ct["tk_g"] = tk_g

    def c_head(tt):
        c_head_gates(tt)
        c_head_wz(tt)

    def c_stats(tt):
        tk_g = CT[tt]["tk_g"]
        for oc in range(8):
            sq = Ar[:, OB: OB + 512]
            tk_sq_w = S.emit("scalar", lambda e, tt=tt, oc=oc, sq=sq: e.activation(
                sq, zsl_ap(tt, oc), AF.Square), (tk_g[oc], gt_state["zsq"]), (SC, 1))
            S.emit("tensor", lambda e, sq=sq, oc=oc: e.matmul(
                pS2[0:33, :], Ar[:, ST_ZO: ST_ZO + 33], sq,
                start=(oc == 0), stop=(oc == 7)),
                (tk_sq_w,) + ((stat_free["pS2"],) if oc == 0 else ()), (T, 1))
            tk_sq2 = S.emit("tensor", lambda e, sq=sq, oc=oc: e.matmul(
                pBC[0:33, :], Ar[:, ST_WSQ + oc * 33: ST_WSQ + oc * 33 + 33], sq,
                start=(oc == 0), stop=(oc == 7)),
                (tk_wsq,) + ((stat_free["pBC"],) if oc == 0 else ()), (T, 1))
            gt_state["zsq"] = tk_sq2
            S.emit("tensor", lambda e, tt=tt, oc=oc: e.matmul(
                pS[0:33, :], Ar[:, ST_WS + oc * 33: ST_WS + oc * 33 + 33],
                zsl_ap(tt, oc), start=(oc == 0), stop=(oc == 7)),
                (tk_g[oc], tk_c["ws"]) + ((stat_free["pS"],) if oc == 0 else ()),
                (T, 1))
        CT[tt]["tk_stats"] = (T, S.cnt[T])

    ct_scal = {"t": ()}

    def c_tail(tt):
        ct = CT[tt]
        S.emit("vector", lambda e: e.tensor_single_scalar(
            row(0), pS2[32:33, :], 1.0 / D, ALU.mult),
            (ct["tk_stats"],) + gt_state["prevbc"], (V, 1))
        S.emit("scalar", lambda e: e.activation(
            row(1), row(0), AF.Ln, bias=ep6()), ((V, S.cnt[V]), tk_ep6), (SC, 1))
        tk_ln1 = (SC, S.cnt[SC])
        S.emit("scalar", lambda e: e.activation(
            row(2), row(1), AF.Exp, scale=-0.5), (tk_ln1,), (SC, 1))
        tk_rms = (SC, S.cnt[SC])
        S.emit("vector", lambda e: e.tensor_single_scalar(
            row(0), pS[32:33, :], 1.0 / D, ALU.mult), (tk_ln1,), None)
        S.emit("vector", lambda e: e.tensor_mul(row(1), row(0), row(2)),
               (tk_rms,), None)
        S.emit("vector", lambda e: e.tensor_single_scalar(
            row(0), pBC[32:33, :], 1.0 / D, ALU.mult), (), (V, 1))
        tk_rowmid = (V, S.cnt[V])
        stat_free["pS"] = tk_rowmid
        stat_free["pS2"] = tk_rowmid
        stat_free["pBC"] = tk_rowmid
        S.emit("vector", lambda e: e.tensor_mul(row(0), row(0), row(2)), (), None)
        S.emit("vector", lambda e: e.tensor_mul(row(0), row(0), row(2)), (), None)
        S.emit("vector", lambda e: e.tensor_mul(row(3), row(1), row(1)), (), None)
        S.emit("vector", lambda e: e.tensor_sub(row(0), row(0), row(3)), (), (V, 1))
        S.emit("scalar", lambda e: e.activation(
            row(3), row(0), AF.Ln, bias=ep5()), ((V, S.cnt[V]), tk_ep5), (SC, 1))
        S.emit("scalar", lambda e: e.activation(
            row(0), row(3), AF.Exp, scale=-0.5), ((SC, S.cnt[SC]),), (SC, 1))
        tk_rstd2 = (SC, S.cnt[SC])
        S.emit("vector", lambda e: e.tensor_mul(row(3), row(2), row(0)),
               (tk_rstd2,), (V, 1))
        tk_alpha = (V, S.cnt[V])
        ct_scal["t"] = (tk_rms, tk_rstd2)
        S.emit("vector", lambda e: e.tensor_mul(row(1), row(1), row(0)), (), (V, 1))
        tk_beta = (V, S.cnt[V])
        S.emit("tensor", lambda e: e.matmul(
            pBC2[:, :], Ar[32:33, ST_ON: ST_ON + 128], row(3), start=True,
            stop=True), (tk_alpha, gt_state["pBC2"]), (T, 1))
        tk_abc = (T, S.cnt[T])
        S.emit("tensor", lambda e: e.matmul(
            pS[:, :], Ar[32:33, ST_ON: ST_ON + 128], row(1), start=True,
            stop=True), (tk_beta, stat_free["pS"]), (T, 1))
        tk_bbc = (T, S.cnt[T])
        S.emit("vector", lambda e, tt=tt: e.tensor_copy(ab_ap(tt, "a"), pBC2[:, :]),
               (tk_abc,), (V, 1))
        tk_acp = (V, S.cnt[V])
        S.emit("vector", lambda e, tt=tt: e.tensor_copy(ab_ap(tt, "b"), pS[:, :]),
               (tk_bbc,), (V, 1))
        tk_bcp = (V, S.cnt[V])
        gt_state["pBC2"] = tk_acp
        stat_free["pS"] = tk_bcp
        gt_state["prevbc"] = (tk_bbc,)
        for oc in range(8):
            S.emit("vector", lambda e, tt=tt, oc=oc: e.tensor_mul(
                Ar[:, GT1: GT1 + 512], zsl_ap(tt, oc), ab_ap(tt, "a")),
                (tk_acp, gt_state["gt1"][0]), (V, 1))
            tk_t1 = (V, S.cnt[V])
            S.emit("vector", lambda e, oc=oc, tt=tt: e.scalar_tensor_tensor(
                Ar[:, GT2: GT2 + 512], Ar[:, GT1: GT1 + 512],
                Ar[:, ST_RWT + oc: ST_RWT + oc + 1], ab_ap(tt, "b"),
                ALU.mult, ALU.subtract), (tk_bcp, tk_c["rw"], gt_state["gt2"][0]),
                (V, 1))
            tk_t2 = (V, S.cnt[V])
            gt_state["gt1"][0] = tk_t2
            xsl = xw_sl(oc, tt * ATILE, 512)
            S.emit("vector", lambda e, xsl=xsl: e.tensor_add(
                xsl, Ar[:, GT2: GT2 + 512], xsl), (ct["tk_gate_last"],), (V, 1))
            tk_r1o = (V, S.cnt[V])
            gt_state["gt2"][0] = tk_r1o
            z_free[(tt, oc)] = tk_r1o
            r1_tok[(oc, tt)] = tk_r1o

    lam_b = Ar[:, LAMC: LAMC + 256]
    bx_b = Ar[:, BXC: BXC + 256]
    tk_h2 = S.emit("vector", lambda e, lam_b=lam_b, bx_b=bx_b:
                   e.tensor_tensor_scan(h_b, lam_b, bx_b, 0.0,
                                        ALU.mult, ALU.add),
                   tuple(lam_toks) + tuple(tk_scan1) + (("QG1", 256),), (V, 1))
    tk_p2 = S.emit("vector", lambda e, lam_b=lam_b:
                   e.tensor_tensor_scan(P_b, lam_b, zero_b, 1.0,
                                        ALU.mult, ALU.add),
                   (("QG1", 256),), (V, 1))
    tk_f3 = S.emit("scalar", lambda e: e.copy(
        Ar[:, BXC + 1264: BXC + 1280], Ar[:, BXC + 1264: BXC + 1280]),
        (tk_h2,), (SC, 1))
    tk_f4 = S.emit("scalar", lambda e: e.copy(
        Ar[:, BXC + 1520: BXC + 1536], Ar[:, BXC + 1520: BXC + 1536]),
        (tk_p2,), (SC, 1))
    for c_ in range(8):
        dma("scalar", "QG1", hl8[:, c_: c_ + 1],
            Ar[16 * c_: 16 * c_ + 16, BXC + 1279: BXC + 1280],
            (tk_f3,) + tuple(tk_scan1))
        dma("scalar", "QG1", pl8[:, c_: c_ + 1],
            Ar[16 * c_: 16 * c_ + 16, BXC + 1535: BXC + 1536],
            (tk_f4,) + tuple(tk_scan1))
    tk_cb2 = S.emit("vector", lambda e: e.tensor_tensor_scan(
        compb, pl8, hl8, Ar[0:16, ST_CAR: ST_CAR + 1], ALU.mult, ALU.add),
        (("QG1", S.cnt["QG1"]),) + tuple(tk_scan1), (V, 1))
    tk_fc = S.emit("scalar", lambda e: e.copy(compb, compb), (tk_cb2,), (SC, 1))
    tk_c0 = S.emit("scalar", lambda e: e.copy(
        carr_bc[0:16, :], Ar[0:16, ST_CAR: ST_CAR + 1]),
        tuple(tk_scan1), (SC, 1))
    for c_ in range(1, 8):
        dma("scalar", "QG2", carr_bc[16 * c_: 16 * c_ + 16, :],
            compb[:, c_ - 1: c_], (tk_fc,))
    tk_fix = S.emit("vector", lambda e: e.scalar_tensor_tensor(
        h_b, P_b, carr_bc, h_b, ALU.mult, ALU.add),
        (("QG2", S.cnt["QG2"]), tk_c0, tk_f3, tk_f4), (V, 1))
    for c_ in range(8):
        dma("scalar", "QU", Ar[0:16, HSC + c_ * 256: HSC + c_ * 256 + 256],
            Ar[16 * c_: 16 * c_ + 16, BXC + 1024: BXC + 1280], (tk_fix,))
    tk_scan2 = ("QU", S.cnt["QU"])
    c_head_gates(0)
    c_head_wz(0)
    c_stats(0)
    for tt in range(1, NAT):
        c_head(tt)
        c_tail(tt - 1)
        c_stats(tt)
    c_tail(NAT - 1)

    if "r1" in dict(dbg):
        S.emit("gpsimd", lambda e: e.dma_start(
            out=dbg_h["r1"], in_=Ar[:, XW: XW + KC * TW]),
            tuple(r1_tok.values()), ("GD", 16))
    if "hs" in dict(dbg):
        S.emit("gpsimd", lambda e: e.dma_start(
            out=dbg_h["hs"], in_=Ar[0:16, HSC: HSC + 2048]), (tk_scan2,), ("GD", 16))

    hbank = [pA, pB]
    h_free = [(g_free[0],), (g_free[1],)]
    w1b_free = [z_free[(2, 1)], z_free[(2, 3)], z_free[(3, 7)]]
    w2b_free = [z_free[(2, 5)], z_free[(2, 7)], z_free[(3, 7)]]
    W1B = [ZB, ZB + 1024, HSC]
    W2B = [ZB + 2048, ZB + 3072, HSC + 1024]
    y2bank = [pC, pD, pBC, pBC2]
    y2_free = [y_free[0], y_free[1], stat_free["pBC"], gt_state["pBC2"]]
    sqD_free = [gt_state["zsq"], gt_state["zsq"]]
    DT = {}
    lastz = tuple(z_free[(3, oc)] for oc in range(8))

    def y2s_ap(oc):
        return Ar[:, LAMC + oc * 256: LAMC + oc * 256 + 256]

    def xfh_ap(st, kc):
        b = Ar[:, XPB + (st % 2) * 2048: XPB + (st % 2) * 2048 + 1024].bitcast(BF16)
        return b[:, kc * 256: kc * 256 + 256]

    def xfl_ap(st, kc):
        b = Ar[:, XPB + (st % 2) * 2048 + 1024: XPB + (st % 2) * 2048 + 2048].bitcast(BF16)
        return b[:, kc * 256: kc * 256 + 256]

    def row3(i):
        return Ar[32:33, RW + i * 512: RW + i * 512 + 256]

    def row4(i):
        return Ar[32:33, RW + i * 512 + 256: RW + i * 512 + 512]

    n = DTILE

    ln3_state = {"ln": None}

    def d_ln3(st):
        tt = st // 2
        def r1_sl(kc, off=st * DTILE):
            return xw_sl(kc, off, DTILE)
        for kc in range(KC):
            S.emit("tensor", lambda e, kc=kc, r1_sl=r1_sl: e.matmul(
                pS[0:33, 0:n], Ar[:, ST_WR + kc * 33: ST_WR + kc * 33 + 33],
                r1_sl(kc), start=(kc == 0), stop=(kc == KC - 1)),
                (tk_c["wr"], r1_tok[(kc, tt)]) +
                ((DT[st]["pS_free"],) if kc == 0 else ()),
                (T, 1) if kc == KC - 1 else None)
        tk_wr = (T, S.cnt[T])
        for kc in range(KC):
            par = kc % 2
            sq = Ar[:, OB + par * 256: OB + par * 256 + 256]
            tk_sq_w = S.emit("scalar", lambda e, sq=sq, kc=kc, r1_sl=r1_sl: e.activation(
                sq, r1_sl(kc), AF.Square), (r1_tok[(kc, tt)], sqD_free[par]), (SC, 1))
            tk_p = S.emit("tensor", lambda e, sq=sq, kc=kc: e.matmul(
                pS2[0:33, 0:n], Ar[:, ST_ZO: ST_ZO + 33], sq,
                start=(kc == 0), stop=(kc == KC - 1)),
                (tk_sq_w,) + ((DT[st]["pS2_free"],) if kc == 0 else ()), (T, 1))
            sqD_free[par] = tk_p
        tk_sqg = (T, S.cnt[T])
        lg = Ar[0:8, RW + (st % 2) * 512: RW + (st % 2) * 512 + n]
        S.emit("vector", lambda e: e.tensor_single_scalar(
            row3(0), pS[32:33, 0:n], -1.0 / D, ALU.mult),
            (tk_wr,) + DT[st]["rowguard3"], None)
        S.emit("vector", lambda e, lg=lg: e.tensor_copy(lg, pS[0:8, 0:n]),
               (tk_wr,) + DT[st]["lgguard"], None)
        S.emit("vector", lambda e: e.tensor_single_scalar(
            row3(1), pS2[32:33, 0:n], 1.0 / D, ALU.mult),
            (tk_sqg, ln3_state["ln"]), None)
        S.emit("vector", lambda e: e.tensor_mul(row3(2), row3(0), row3(0)), (), None)
        S.emit("vector", lambda e: e.tensor_sub(row3(1), row3(1), row3(2)), (), (V, 1))
        S.emit("scalar", lambda e: e.activation(
            row3(2), row3(1), AF.Ln, bias=ep5()), ((V, S.cnt[V]), tk_ep5), (SC, 1))
        ln3_state["ln"] = (SC, S.cnt[SC])
        S.emit("scalar", lambda e: e.activation(
            row3(3), row3(2), AF.Exp, scale=-0.5), (ln3_state["ln"],), (SC, 1))
        tk_rstd3 = (SC, S.cnt[SC])
        S.emit("vector", lambda e: e.tensor_mul(row3(0), row3(0), row3(3)),
               (tk_rstd3,), (V, 1))
        tk_nmr3 = (V, S.cnt[V])
        DT[st].update(tk_wr=tk_wr, tk_rstd3=tk_rstd3, tk_nmr3=tk_nmr3, lg=lg,
                      rowread3=(V, S.cnt[V]))

    def d_bc3_smax_xf(st):
        d = DT[st]
        tt = st // 2
        def r1_sl(kc, off=st * DTILE):
            return xw_sl(kc, off, DTILE)
        lg = d["lg"]
        S.emit("tensor", lambda e: e.matmul(
            pS2[:, 0:n], Ar[32:33, ST_ON: ST_ON + 128], row3(3),
            start=True, stop=True), (d["tk_rstd3"], d["rowread3"]), (T, 1))
        tk_bc3r = (T, S.cnt[T])
        S.emit("tensor", lambda e: e.matmul(
            pS[:, 0:n], Ar[32:33, ST_ON: ST_ON + 128], row3(0),
            start=True, stop=True), (d["tk_nmr3"],), (T, 1))
        tk_bc3n = (T, S.cnt[T])
        S.emit("vector", lambda e, lg=lg: e.tensor_mul(
            lg, lg, pS2[0:8, 0:n]), (tk_bc3r,), (V, 1))
        S.emit("vector", lambda e, lg=lg: e.scalar_tensor_tensor(
            lg, pS[0:8, 0:n], Ar[0:8, ST_WSR: ST_WSR + 1], lg,
            ALU.mult, ALU.add), (tk_wsr, tk_bc3n), (V, 1))
        tk_lg = (V, S.cnt[V])
        tk_xf = {}
        xtmp = Ar[:, BXC + 512: BXC + 512 + 256]
        for kc in range(KC):
            S.emit("vector", lambda e, kc=kc, xtmp=xtmp, r1_sl=r1_sl: e.tensor_mul(
                xtmp, r1_sl(kc), pS2[:, 0:n]),
                (tk_bc3r,) + d["xfguard"], (V, 1))
            S.emit("vector", lambda e, xtmp=xtmp: e.tensor_add(
                xtmp, xtmp, pS[:, 0:n]), (tk_bc3n,), (V, 1))
            S.emit("vector", lambda e, kc=kc, st=st, xtmp=xtmp: e.tensor_copy(
                xfh_ap(st, kc), xtmp), (), (V, 1))
            S.emit("vector", lambda e, kc=kc, st=st, xtmp=xtmp: e.tensor_tensor(
                xfl_ap(st, kc), xtmp, xfh_ap(st, kc), ALU.subtract), (), (V, 1))
            tk_xf[kc] = (V, S.cnt[V])
        eg = Ar[0:8, RW + 1024 + (st % 2) * 256: RW + 1024 + (st % 2) * 256 + n]
        S.emit("scalar", lambda e, lg=lg, eg=eg: e.activation(
            eg, lg, AF.Exp), (tk_lg,), (SC, 1))
        tk_eg = (SC, S.cnt[SC])
        S.emit("tensor", lambda e, eg=eg: e.matmul(
            pB[0:33, 0:n], Ar[0:8, ST_ZO: ST_ZO + 33], eg,
            start=True, stop=True),
            (tk_eg,) + tuple(h_free[1]), (T, 1))
        S.emit("tensor", lambda e, eg=eg: e.matmul(
            pA[0:33, 0:n], Ar[0:8, ST_ZSEL: ST_ZSEL + 33], eg,
            start=True, stop=True),
            (tk_eg, tk_c["zs"]) + tuple(h_free[0]), (T, 1))
        tk_se = (T, S.cnt[T])
        sum4 = Ar[32:33, SUM4C + (st % 2) * 256: SUM4C + (st % 2) * 256 + n]
        num4 = Ar[32:33, NUM4C + (st % 2) * 256: NUM4C + (st % 2) * 256 + n]
        w_s4 = (DT[st - 2]["tk_lnp"],) if st >= 2 and "tk_lnp" in DT.get(st - 2, {}) \
            else ()
        S.emit("vector", lambda e, sum4=sum4: e.tensor_copy(sum4, pB[32:33, 0:n]),
               ((T, S.cnt[T] - 1),) + w_s4, (V, 1))
        S.emit("vector", lambda e, num4=num4: e.tensor_copy(num4, pA[32:33, 0:n]),
               (tk_se,), (V, 1))
        tk_sec = (V, S.cnt[V])
        d.update(tk_xf=tk_xf, tk_sec=tk_sec, sum4=sum4, num4=num4)
        h_free[0] = (tk_sec,)
        h_free[1] = (tk_sec,)
        DT[st + 1] = dict(pS_free=tk_xf[KC - 1], pS2_free=tk_xf[KC - 1],
                          rowguard3=(tk_bc3r, tk_bc3n), lgguard=(tk_lg,),
                          xfguard=(), p7guard=())

    def d_W1(st):
        d = DT[st]
        for fc in range(FC):
            par = fc % 3
            parh = fc % 2
            w1b = Ar[:, W1B[par]: W1B[par] + 1024].bitcast(BF16)
            tk_w1l = dma("sync", ("Q1E", "Q1O", "Q1T")[par],
                         w1b, w1s_h[fc * 128: fc * 128 + 128, :],
                         (w1b_free[par],))
            for kc in range(KC):
                for p3 in range(3):
                    b0 = (kc // 4) * 1024 + (0 if p3 < 2 else 512) \
                        + (kc % 4) * 128
                    wsl = w1b[:, b0: b0 + 128]
                    xsl = (xfh_ap(st, kc) if p3 != 1 else xfl_ap(st, kc))
                    S.emit("tensor", lambda e, wsl=wsl, xsl=xsl, par=parh,
                           kc=kc, p3=p3: e.matmul(
                        hbank[par][:, 0:n], wsl, xsl,
                        start=(kc == 0 and p3 == 0),
                        stop=(kc == KC - 1 and p3 == 2)),
                        ((tk_w1l,) + tuple(h_free[parh]) + tuple(d["tk_xf"].values())
                         if kc == 0 and p3 == 0 else ()),
                        (T, 1) if kc == KC - 1 and p3 == 2 else None)
            tk_h = (T, S.cnt[T])
            w1b_free[par] = tk_h
            tk_gelu = S.emit("scalar", lambda e, fc=fc, par=parh: e.activation(
                Ar[:, WG + fc * 256: WG + fc * 256 + 256], hbank[par][:, 0:n],
                AF.Gelu, bias=Ar[:, ST_B1 + fc: ST_B1 + fc + 1]),
                (tk_h, tk_c["b1"], d["h_cons"]), (SC, 1))
            h_free[parh] = (tk_gelu,)
            d.setdefault("gelu", {})[fc] = tk_gelu

    def d_W2(st):
        d = DT[st]
        for fc in range(FC):
            par = fc % 3
            w2b = Ar[:, W2B[par]: W2B[par] + 1024]
            tk_w2l = dma("gpsimd", ("Q2E", "Q2O", "Q2T")[par],
                         w2b, w2_h[fc * 128: fc * 128 + 128, :],
                         (w2b_free[par],))
            for oc in range(8):
                bank = y2bank[oc // 2]
                half = (oc % 2) * 256
                wts = (tk_w2l, d["gelu"][fc]) if oc == 0 else ()
                if fc == 0 and oc % 2 == 0:
                    wts = wts + (y2_free[oc // 2],)
                    if bank in (pBC, pBC2):
                        wts = wts + tuple(d["tk_xf"].values())
                S.emit("tensor",
                       lambda e, w2b=w2b, bank=bank, half=half,
                       fc=fc, oc=oc: e.matmul(
                           bank[:, half: half + n],
                           w2b[:, oc * 128: oc * 128 + 128],
                           Ar[:, WG + fc * 256: WG + fc * 256 + 256],
                           start=(fc == 0 and oc % 2 == 0),
                           stop=(fc == FC - 1),
                           skip_group_check=True),
                       wts, (T, 1) if oc == 7 else None)
            tk_w2fc = (T, S.cnt[T])
            w2b_free[par] = tk_w2fc
        d["tk_w2last"] = tk_w2fc

    def d_LN4(st):
        d = DT[st]
        tk_y2 = {}
        for oc in range(8):
            bank = y2bank[oc // 2]
            half = (oc % 2) * 256
            if oc % 2 == 0:
                S.emit("scalar", lambda e, bank=bank, half=half, oc=oc:
                       e.activation(
                           y2s_ap(oc), bank[:, half: half + n], AF.Identity,
                           bias=Ar[:, ST_B2 + oc: ST_B2 + oc + 1]),
                       (d["tk_w2last"], tk_c["b2"]) + d["y2guard"],
                       (SC, 1))
                tk_y2[oc] = (SC, S.cnt[SC])
            else:
                S.emit("vector", lambda e, bank=bank, half=half, oc=oc:
                       e.tensor_single_scalar(
                           y2s_ap(oc), bank[:, half: half + n],
                           Ar[:, ST_B2 + oc: ST_B2 + oc + 1], ALU.add),
                       (d["tk_w2last"], tk_c["b2"]) + d["y2guard"],
                       (V, 1))
                tk_y2[oc] = (V, S.cnt[V])
        for oc in range(8):
            par = oc % 2
            S.emit("tensor", lambda e, oc=oc: e.matmul(
                pC[0:33, 0:n], Ar[:, ST_ZO: ST_ZO + 33], y2s_ap(oc),
                start=(oc == 0), stop=(oc == 7)),
                (tk_y2[oc],) + ((tk_y2[1],) if oc == 0 else ()), (T, 1))
            sq = Ar[:, OB + par * 256: OB + par * 256 + 256]
            tk_sq_w = S.emit("scalar", lambda e, oc=oc, sq=sq: e.activation(
                sq, y2s_ap(oc), AF.Square), (tk_y2[oc], sqD_free[par]), (SC, 1))
            tk_p = S.emit("tensor", lambda e, sq=sq, oc=oc: e.matmul(
                pD[0:33, 0:n], Ar[:, ST_ZO: ST_ZO + 33], sq,
                start=(oc == 0), stop=(oc == 7)),
                (tk_sq_w,) + ((tk_y2[3],) if oc == 0 else ()), (T, 1))
            sqD_free[par] = tk_p
        tk_s4 = (T, S.cnt[T])
        sum4 = d["sum4"]
        num4 = d["num4"]
        S.emit("vector", lambda e: e.tensor_single_scalar(
            row4(0), pC[32:33, 0:n], -1.0 / D, ALU.mult),
            (tk_s4,) + d.get("rowguard4", ()), None)
        S.emit("vector", lambda e: e.tensor_single_scalar(
            row4(1), pD[32:33, 0:n], 1.0 / D, ALU.mult), (), None)
        S.emit("vector", lambda e: e.tensor_mul(row4(2), row4(0), row4(0)), (), None)
        S.emit("vector", lambda e: e.tensor_sub(row4(1), row4(1), row4(2)), (), (V, 1))
        tk_mu2r = (V, S.cnt[V])
        S.emit("scalar", lambda e, sum4=sum4: e.activation(
            row4(2), sum4, AF.Ln), (d["tk_sec"], tk_mu2r), (SC, 1))
        d["tk_lnp"] = (SC, S.cnt[SC])
        S.emit("scalar", lambda e: e.activation(
            row4(2), row4(2), AF.Exp, scale=-1.0), (d["tk_lnp"],), (SC, 1))
        tk_p7 = (SC, S.cnt[SC])
        S.emit("vector", lambda e, num4=num4: e.tensor_mul(row4(2), num4, row4(2)),
               (tk_p7,), None)
        S.emit("vector", lambda e: e.tensor_mul(row4(1), row4(1), row4(2)), (), None)
        S.emit("vector", lambda e: e.tensor_mul(row4(1), row4(1), row4(2)), (), (V, 1))
        tk_var4 = (V, S.cnt[V])
        S.emit("scalar", lambda e: e.activation(
            row4(3), row4(1), AF.Ln, bias=ep5()), (tk_var4, tk_ep5), (SC, 1))
        S.emit("scalar", lambda e: e.activation(
            row4(3), row4(3), AF.Exp, scale=-0.5), ((SC, S.cnt[SC]),), (SC, 1))
        tk_rsd4 = (SC, S.cnt[SC])
        S.emit("vector", lambda e: e.tensor_mul(row4(3), row4(3), row4(2)),
               (tk_rsd4,), (V, 1))
        tk_rstd4 = (V, S.cnt[V])
        S.emit("vector", lambda e: e.tensor_mul(row4(0), row4(0), row4(3)), (), (V, 1))
        tk_nmr4 = (V, S.cnt[V])
        d.update(tk_y2=tk_y2, tk_rstd4=tk_rstd4, tk_nmr4=tk_nmr4,
                 rowread4=(V, S.cnt[V]))

    def d_out(st):
        d = DT[st]
        off = st * DTILE
        S.emit("tensor", lambda e: e.matmul(
            pBC[:, 0:n], Ar[32:33, ST_ON: ST_ON + 128], row4(3),
            start=True, stop=True),
            (d["tk_rstd4"], d["rowread4"], d["tk_y2"][5]), (T, 1))
        tk_bc4r = (T, S.cnt[T])
        S.emit("tensor", lambda e: e.matmul(
            pBC2[:, 0:n], Ar[32:33, ST_ON: ST_ON + 128], row4(0),
            start=True, stop=True), (d["tk_nmr4"], d["tk_y2"][7]), (T, 1))
        tk_bc4n = (T, S.cnt[T])
        obp_free = d["obp_free"]
        tk_rowfree = None
        for oc in range(8):
            par = oc % 2
            ot = Ar[:, GT2 + par * 256: GT2 + par * 256 + 256]
            S.emit("vector", lambda e, ot=ot, oc=oc: e.tensor_mul(
                ot, y2s_ap(oc), pBC[:, 0:n]), (tk_bc4r, obp_free[par]), (V, 1))
            S.emit("vector", lambda e, ot=ot: e.tensor_add(
                ot, ot, pBC2[:, 0:n]), (tk_bc4n,), None)
            S.emit("vector", lambda e, ot=ot, oc=oc, off=off: e.tensor_add(
                ot, ot, xw_sl(oc, off, DTILE)), (), (V, 1))
            tk_o = (V, S.cnt[V])
            tk_rowfree = tk_o
            tk_st = S.emit("sync", lambda e, ot=ot, oc=oc, off=off: e.dma_start(
                out=out_h[:, oc * TW + off: oc * TW + off + DTILE], in_=ot),
                (tk_o,), ("GOE" if par == 0 else "GOO", 16))
            obp_free[par] = tk_st
            out_toks.append(tk_st)
        d["tk_out"] = tk_rowfree
        return dict(pBC=tk_rowfree, pBC2=tk_rowfree,
                    pC=d["rowread4"], pD=d["rowread4"])

    out_toks = []
    if skip_d:
        S.emit("gpsimd", lambda e: e.engine_nop(),
               tuple(r1_tok.values()), None)
    DT[0] = dict(pS_free=stat_free["pS"], pS2_free=stat_free["pS2"],
                 rowguard3=ct_scal["t"], rowguard4=ct_scal["t"],
                 lgguard=(), xfguard=(), p7guard=())
    DT[0]["h_cons"] = None
    DT[0]["y2guard"] = lastz
    DT[0]["obp_free"] = [gt_state["gt2"][0], gt_state["gt2"][0]]
    if not skip_d:
        d_ln3(0)
        d_bc3_smax_xf(0)
    for st in (range(NDT) if not skip_d else ()):
        d_W1(st)
        if st >= 1:
            bank_toks = d_out(st - 1)
            y2_free = [bank_toks["pC"], bank_toks["pD"],
                       bank_toks["pBC"], bank_toks["pBC2"]]
        if st < NDT - 1:
            DT[st + 1]["h_cons"] = None
            d_ln3(st + 1)
        d_W2(st)
        if st < NDT - 1:
            d_bc3_smax_xf(st + 1)
            DT[st + 2]["xfguard"] = ((T, S.cnt[T]),)
            DT[st + 2]["p7guard"] = ()
        d_LN4(st)
        if st < NDT - 1:
            DT[st + 1]["h_cons"] = DT[st]["tk_w2last"]
            DT[st + 1]["y2guard"] = tuple(DT[st]["tk_y2"].values()) + (
                (DT[st]["tk_out"],) if "tk_out" in DT[st] else ())
            DT[st + 1]["obp_free"] = DT[st]["obp_free"]
    if not skip_d:
        d_out(NDT - 1)

    S.emit("gpsimd", lambda e: e.engine_nop(),
           (("GOE", S.cnt.get("GOE", 0)), ("GOO", S.cnt.get("GOO", 0))), None)

    sems = {}
    sem_names = ["T", "V", "SC", "QC", "QP", "Q1E", "Q1O", "Q2E", "Q2O",
                 "GOE", "GOO", "GD", "QW0", "QW1", "QW2", "QW3",
                 "GW0", "GW1", "GW2", "GW3", "GC", "Q1T", "Q2T",
                 "QLB", "QG1", "QG2", "QU", "QX", "QG"]
    for snm in sem_names:
        sems[snm] = nc.alloc_semaphore("sem" + snm)
    with (
        nc.Block() as block,
    ):

        def replay(name):
            def body(eng):
                last = {}
                for fn, waits, inc in S.ops[name]:
                    for (s, v) in waits:
                        if last.get(s, 0) < v:
                            eng.wait_ge(sems[s], v)
                            last[s] = v
                    inst = fn(eng)
                    if inc is not None:
                        inst.then_inc(sems[inc[0]], inc[1])
            return body

        block.sync(replay("sync"))
        block.gpsimd(replay("gpsimd"))
        block.scalar(replay("scalar"))
        block.vector(replay("vector"))
        block.tensor(replay("tensor"))
    return nc


def make_in_maps(inputs):
    x = np.asarray(inputs["L0_attn_x"], np.float32)
    Wl = np.asarray(inputs["L0_attn_W_lambda"], np.float32)
    bl = np.asarray(inputs["L0_attn_b_lambda"], np.float32)
    Wb = np.asarray(inputs["L0_attn_W_B"], np.float32)
    Wc = np.asarray(inputs["L0_attn_W_C"], np.float32)
    Wgt = np.asarray(inputs["L0_attn_W_gate"], np.float32)
    rw = np.asarray(inputs["L0_attn_rms_weight"], np.float32)
    Wr = np.asarray(inputs["L0_ffn_Wr"], np.float32)
    W1 = np.asarray(inputs["L0_ffn_W1"], np.float32)
    b1 = np.asarray(inputs["L0_ffn_b1"], np.float32)
    W2 = np.asarray(inputs["L0_ffn_W2"], np.float32)
    b2 = np.asarray(inputs["L0_ffn_b2"], np.float32)

    ones8 = np.ones((8, 128, 1), np.float32)
    wlb = np.ascontiguousarray(np.concatenate(
        [Wl.T.reshape(8, 128, 16), Wb.T.reshape(8, 128, 16), ones8],
        axis=2).transpose(1, 0, 2).reshape(128, 264))
    perm = [7, 0, 1, 2, 3, 4, 5, 6]
    wr33 = np.zeros((8, 128, 33), np.float32)
    wr33[:, :, 0:8] = Wr[perm].T.reshape(8, 128, 8)
    wr33[:, :, 32] = 1.0
    wr33 = np.ascontiguousarray(wr33.transpose(1, 0, 2).reshape(128, 264))
    ws33 = np.zeros((8, 128, 33), np.float32)
    ws33[:, :, 32] = rw.reshape(8, 128)
    ws33 = np.ascontiguousarray(ws33.transpose(1, 0, 2).reshape(128, 264))
    wg = np.ascontiguousarray(
        Wgt.T.reshape(8, 128, 8, 128).transpose(1, 0, 2, 3).reshape(128, 8192))
    wc = np.ascontiguousarray(Wc.T)
    w1 = np.ascontiguousarray(
        W1[7].reshape(8, 128, 32, 128).transpose(2, 1, 0, 3).reshape(4096, 1024))
    w2 = np.ascontiguousarray(W2[7])
    zo33 = np.zeros((128, 33), np.float32)
    zo33[:, 32] = 1.0
    zsel = np.zeros((128, 33), np.float32)
    zsel[0, 32] = 1.0
    ones = np.ones((128, 128), np.float32)
    b1c = np.ascontiguousarray(b1[7].reshape(32, 128).T)
    b2c = np.ascontiguousarray(b2[7].reshape(8, 128).T)
    rmsw = np.ascontiguousarray(rw.reshape(8, 128).T)
    blam = np.ascontiguousarray(bl.reshape(16, 1))
    zeros_x = np.zeros((128, KC * TW), np.float32)

    def fm(xslice):
        return np.ascontiguousarray(
            xslice.T.reshape(8, 128, TW).transpose(1, 0, 2).reshape(128, KC * TW))

    in_maps = []
    for core in range(NCORES):
        b, h = core // 2, core % 2
        xw = fm(x[b, h * TW:(h + 1) * TW, :])
        xp = fm(x[b, 0:TW, :]) if h == 1 else zeros_x
        xp = np.ascontiguousarray(
            xp.reshape(128, KC, 4, 512).transpose(0, 2, 1, 3)
            .reshape(128, KC * TW))
        in_maps.append({
            "xw": xw, "xp": xp, "wlb": wlb, "wr": wr33, "ws33": ws33,
            "wg": wg, "wc": wc, "w1": w1, "w2": w2, "zo33": zo33, "zsel": zsel,
            "ones": ones, "b1c": b1c, "b2c": b2c, "rmsw": rmsw, "b_lam": blam,
        })
    return in_maps


def kernel(**inputs):
    in_maps = make_in_maps(inputs)
    nc = build_nc()
    res = run_bass_kernel_spmd(nc, in_maps, core_ids=list(range(NCORES)))
    out = np.empty((4, 4096, 1024), np.float32)
    for core in range(NCORES):
        b, h = core // 2, core % 2
        o = np.asarray(res.results[core]["out"])
        o = o.reshape(128, 8, TW).transpose(1, 0, 2).reshape(1024, TW)
        out[b, h * TW:(h + 1) * TW, :] = o.T
    return out


# version 44
# speedup vs baseline: 1.0940x; 1.0940x over previous
import numpy as np
import concourse.bass as bass
import concourse.mybir as mybir
from concourse.bass_utils import run_bass_kernel_spmd

F32 = mybir.dt.float32
BF16 = mybir.dt.bfloat16
AF = mybir.ActivationFunctionType
ALU = mybir.AluOpType

NCORES = 8
D, KC, NST = 1024, 8, 16
TW = 2048
ATILE = 512
NAT = TW // ATILE
DTILE = 256
NDT = TW // DTILE
F, FC = 4096, 32
EPS_LN = 1e-5
EPS_RMS = 1e-6

XW = 0
WG = 16384
XPB = 24576
ZB = 28672
LAMC = 32768
BXC = 34816
HSC = 36864
ST = 38912
BC1 = 41472
GT1 = 45568
GT2 = 46080
OB = 46592
RW = 47104
SUM4C = 49152
NUM4C = 49664
ACOLS = 50176

ST_WLB = ST
ST_WR = ST + 264
ST_WS = ST + 528
ST_WSQ = ST + 792
ST_WC = ST + 1056
ST_ZO = ST + 2080
ST_ON = ST + 2113
ST_B1 = ST + 2241
ST_B2 = ST + 2273
ST_RWT = ST + 2281
ST_BL = ST + 2289
ST_WSL = ST + 2290
ST_WSR = ST + 2291
ST_WGS = ST + 2292
ST_CAR = ST + 2300
ST_ZSEL = ST + 2303
ST_EP5 = ST + 2301
ST_EP6 = ST + 2302

ENGINES = ("sync", "gpsimd", "scalar", "vector", "tensor")


class Sched:
    def __init__(self):
        self.ops = {e: [] for e in ENGINES}
        self.cnt = {}

    def emit(self, eng, fn, waits=(), inc=None):
        tok = None
        if inc is not None:
            s, a = inc
            self.cnt[s] = self.cnt.get(s, 0) + a
            tok = (s, self.cnt[s])
        self.ops[eng].append((fn, tuple(w for w in waits if w is not None), inc))
        return tok


def build_nc(dbg=(), skip_d=False):
    nc = bass.Bass("TRN2", target_bir_lowering=False,
                   detect_race_conditions=False)

    xw_h = nc.dram_tensor("xw", [128, KC * TW], F32, kind="ExternalInput").ap()
    xp_h = nc.dram_tensor("xp", [128, KC * TW], F32, kind="ExternalInput").ap()
    wlb_h = nc.dram_tensor("wlb", [128, KC * 33], F32, kind="ExternalInput").ap()
    wr_h = nc.dram_tensor("wr", [128, KC * 33], F32, kind="ExternalInput").ap()
    ws_h = nc.dram_tensor("ws33", [128, 8 * 33], F32, kind="ExternalInput").ap()
    wg_h = nc.dram_tensor("wg", [128, 8192], F32, kind="ExternalInput").ap()
    wc_h = nc.dram_tensor("wc", [16, 1024], F32, kind="ExternalInput").ap()
    w1_h = nc.dram_tensor("w1", [FC * 128, 1024], F32, kind="ExternalInput").ap()
    w2_h = nc.dram_tensor("w2", [FC * 128, 1024], F32, kind="ExternalInput").ap()
    zo_h = nc.dram_tensor("zo33", [128, 33], F32, kind="ExternalInput").ap()
    zs_h = nc.dram_tensor("zsel", [128, 33], F32, kind="ExternalInput").ap()
    on_h = nc.dram_tensor("ones", [128, 128], F32, kind="ExternalInput").ap()
    b1_h = nc.dram_tensor("b1c", [128, 32], F32, kind="ExternalInput").ap()
    b2_h = nc.dram_tensor("b2c", [128, 8], F32, kind="ExternalInput").ap()
    rw_h = nc.dram_tensor("rmsw", [128, 8], F32, kind="ExternalInput").ap()
    bl_h = nc.dram_tensor("b_lam", [16, 1], F32, kind="ExternalInput").ap()
    out_h = nc.dram_tensor("out", [128, KC * TW], F32, kind="ExternalOutput").ap()
    w1s_h = nc.dram_tensor("w1s", [FC * 128, 2048], BF16, kind="Internal").ap()
    dbg_h = {}
    for name, shape in dbg:
        dbg_h[name] = nc.dram_tensor("dbg_" + name, shape, F32, kind="ExternalOutput").ap()

    Ar = nc.alloc_sbuf_tensor("arena", [128, ACOLS], F32)
    pb = [nc.alloc_psum_tensor(f"pb{i}", [128, 512], F32) for i in range(8)]
    pA, pB, pC, pD, pS, pS2, pBC, pBC2 = pb

    S = Sched()
    T, V, SC, Q, G = "T", "V", "SC", "Q", "G"

    def dma(q, sem, out, in_, waits=()):
        return S.emit(q, lambda e, out=out, in_=in_: e.dma_start(out=out, in_=in_),
                      waits, (sem, 16))

    def xw_sl(kc, c0, n):
        return Ar[:, XW + kc * TW + c0: XW + kc * TW + c0 + n]

    ones_col = Ar[:, ST_ON: ST_ON + 1]

    def row(i, n=512):
        return Ar[32:33, RW + i * 512: RW + i * 512 + n]

    tk_c = {}
    tk_c["wlb"] = dma("sync", "QC", Ar[:, ST_WLB: ST_WLB + 264], wlb_h)
    tk_c["wr"] = dma("sync", "QC", Ar[:, ST_WR: ST_WR + 264], wr_h)
    tk_c["ws"] = dma("sync", "QC", Ar[:, ST_WS: ST_WS + 264], ws_h)
    tk_c["wc"] = dma("sync", "QC", Ar[0:16, ST_WC: ST_WC + 1024], wc_h)
    tk_c["zo"] = dma("sync", "QC", Ar[:, ST_ZO: ST_ZO + 33], zo_h)
    tk_c["zs"] = dma("sync", "QC", Ar[:, ST_ZSEL: ST_ZSEL + 33], zs_h)
    tk_c["on"] = dma("sync", "QC", Ar[:, ST_ON: ST_ON + 128], on_h)
    tk_c["b1"] = dma("sync", "QC", Ar[:, ST_B1: ST_B1 + 32], b1_h)
    tk_c["b2"] = dma("sync", "QC", Ar[:, ST_B2: ST_B2 + 8], b2_h)
    tk_c["rw"] = dma("sync", "QC", Ar[:, ST_RWT: ST_RWT + 8], rw_h)
    tk_c["bl"] = dma("sync", "QC", Ar[0:16, ST_BL: ST_BL + 1], bl_h)
    _qc_all = ("QC", S.cnt["QC"])
    for _k in tk_c:
        tk_c[_k] = _qc_all
    tk_c["xw"] = ("QX", 8 * 16)
    tk_c["wg"] = ("QG", 4 * 16)

    tk_ep5 = S.emit("vector", lambda e: e.memset(
        Ar[:, ST_EP5: ST_EP5 + 1], EPS_LN), (), (V, 1))
    tk_ep6 = S.emit("vector", lambda e: e.memset(
        Ar[:, ST_EP6: ST_EP6 + 1], EPS_RMS), (), (V, 1))
    zero_b = Ar[:, SUM4C: SUM4C + 256]
    tk_z0 = S.emit("vector", lambda e: e.memset(zero_b, 0.0), (), (V, 1))
    h_b = Ar[:, BXC + 1024: BXC + 1280]
    P_b = Ar[:, BXC + 1280: BXC + 1536]
    hl8 = Ar[0:16, BXC + 512: BXC + 520]
    pl8 = Ar[0:16, BXC + 520: BXC + 528]
    compb = Ar[0:16, BXC + 528: BXC + 536]
    carr_bc = Ar[:, BXC + 544: BXC + 545]

    def ep5():
        return Ar[32:33, ST_EP5: ST_EP5 + 1]

    def ep6():
        return Ar[32:33, ST_EP6: ST_EP6 + 1]

    tk_wsq = S.emit("scalar", lambda e: e.activation(
        Ar[:, ST_WSQ: ST_WSQ + 264], Ar[:, ST_WS: ST_WS + 264], AF.Square),
        (tk_c["ws"],), (SC, 1))

    for kc in range(KC):
        S.emit("tensor", lambda e, kc=kc: e.matmul(
            pS[0:33, 0:1], Ar[:, ST_WLB + kc * 33: ST_WLB + kc * 33 + 33],
            ones_col, start=(kc == 0), stop=(kc == KC - 1)),
            (tk_c["wlb"], tk_c["on"]),
            (T, 1) if kc == KC - 1 else None)
    tk_mm = (T, S.cnt[T])
    tk_wsl = S.emit("vector", lambda e: e.tensor_copy(
        Ar[0:33, ST_WSL: ST_WSL + 1], pS[0:33, 0:1]), (tk_mm,), (V, 1))
    for kc in range(KC):
        S.emit("tensor", lambda e, kc=kc: e.matmul(
            pS2[0:33, 0:1], Ar[:, ST_WR + kc * 33: ST_WR + kc * 33 + 33],
            ones_col, start=(kc == 0), stop=(kc == KC - 1)),
            (tk_c["wr"],), (T, 1) if kc == KC - 1 else None)
    tk_mm = (T, S.cnt[T])
    tk_wsr = S.emit("vector", lambda e: e.tensor_copy(
        Ar[0:33, ST_WSR: ST_WSR + 1], pS2[0:33, 0:1]), (tk_mm,), (V, 1))
    tk_wgs = None

    bfree = {"pS": tk_wsl, "pS2": tk_wsr, "pBC": None, "pBC2": None,
             "pA": None, "pB": None, "pC": None, "pD": None}
    tmp32_free = [(), ()]
    sq_free = [None, None]
    xpb_free = None
    lam_toks = []
    lamtmp_free = [None]
    tk_scan1 = None
    TI = {}
    prev_bc = {"r": None, "n": None, "nmr": None, "ln": None}
    sp_cons = [(), (), (), ()]
    sp_wout = [None, None, None, None]
    sp_wb_pend = []

    def emit_split_wb(t, s, tk_lo):
        dst_h = w1s_h
        fc = (t % 64) // 2
        h = t % 2
        hilo = Ar[:, ZB + s * 1024 + 512: ZB + s * 1024 + 1024].bitcast(BF16)
        sp_wout[s] = dma("sync", f"GW{s}",
                         dst_h[fc * 128: fc * 128 + 128,
                               h * 1024: h * 1024 + 1024],
                         hilo, (tk_lo,))

    xp_tok = {}

    def a_head(a):
        prefix = a < 4
        tt = a if prefix else a - 4
        even = (a % 2 == 0)
        pst, pst2 = (pS, pS2) if even else (pC, pD)
        pbr, pbn = (pA, pB) if even else (pBC, pBC2)
        nst, nst2 = ("pS", "pS2") if even else ("pC", "pD")
        nbr, nbn = ("pA", "pB") if even else ("pBC", "pBC2")

        if prefix:
            if a == 0:
                xp_tok[0] = dma("scalar", "QP", Ar[:, XPB: XPB + 4096],
                                xp_h[:, 0: 4096], ())
            tk_src = xp_tok[a]
            def src(kc, a=a):
                base = XPB if a % 2 == 0 else BC1
                return Ar[:, base + kc * 512: base + kc * 512 + 512]
        else:
            tk_src = tk_c["xw"]
            def src(kc, tt=tt):
                return xw_sl(kc, tt * ATILE, 512)

        for kc in range(KC):
            S.emit("tensor", lambda e, kc=kc, src=src, pst=pst: e.matmul(
                pst[0:33, :], Ar[:, ST_WLB + kc * 33: ST_WLB + kc * 33 + 33],
                src(kc), start=(kc == 0), stop=(kc == KC - 1)),
                (tk_src, tk_c["wlb"], bfree[nst]) if kc == 0 else (),
                (T, 1) if kc == KC - 1 else None)
        tk_lb = (T, S.cnt[T])
        for kc in range(KC):
            par = kc % 2
            sqsl = Ar[:, (OB if par == 0 else GT1): (OB if par == 0 else GT1) + 512]
            tk_sq_w = S.emit("vector", lambda e, sqsl=sqsl, kc=kc, src=src:
                             e.tensor_mul(sqsl, src(kc), src(kc)),
                             (tk_src, sq_free[par]), (V, 1))
            tk_pass = S.emit("tensor", lambda e, sqsl=sqsl, kc=kc, pst2=pst2: e.matmul(
                pst2[0:33, :], Ar[:, ST_ZO: ST_ZO + 33], sqsl,
                start=(kc == 0), stop=(kc == KC - 1)),
                (tk_sq_w, tk_c["zo"]) + ((bfree[nst2],) if kc == 0 else ()),
                (T, 1))
            sq_free[par] = tk_pass
        tk_sq = (T, S.cnt[T])
        TI[a] = dict(src=src, tk_lb=tk_lb, tk_sq=tk_sq, pst=pst, pst2=pst2,
                     pbr=pbr, pbn=pbn, nst=nst, nst2=nst2, nbr=nbr, nbn=nbn,
                     tt=tt, prefix=prefix)
        if prefix and a + 1 < 4:
            nb = XPB if (a + 1) % 2 == 0 else BC1
            w = (TI[a - 1]["tk_sq"],) if a >= 1 else ()
            xp_tok[a + 1] = dma("scalar", "QP", Ar[:, nb: nb + 4096],
                                xp_h[:, (a + 1) * 4096: (a + 1) * 4096 + 4096],
                                w)

    for a in range(8):
        if a != 5:
            a_head(a)
        for kc_ in {0: (0, 1), 1: (2, 3, 4), 2: (5, 6, 7)}.get(a, ()):
            dma("sync", "QX", Ar[:, XW + kc_ * TW: XW + kc_ * TW + TW],
                xw_h[:, kc_ * TW: kc_ * TW + TW])
        for q_ in {2: (0, 1), 3: (2, 3)}.get(a, ()):
            dma("sync", "QG", Ar[:, WG + q_ * 2048: WG + q_ * 2048 + 2048],
                wg_h[:, q_ * 2048: q_ * 2048 + 2048])

        def a_tail(j, do_evac=True):
            ti = TI[j]
            pst, pst2, pbr, pbn = ti["pst"], ti["pst2"], ti["pbr"], ti["pbn"]
            tt, prefix = ti["tt"], ti["prefix"]
            S.emit("vector", lambda e, pst=pst: e.tensor_single_scalar(
                row(0), pst[32:33, :], -1.0 / D, ALU.mult),
                (ti["tk_lb"], prev_bc["n"]), None)
            S.emit("vector", lambda e, pst2=pst2: e.tensor_single_scalar(
                row(1), pst2[32:33, :], 1.0 / D, ALU.mult),
                (ti["tk_sq"], prev_bc["ln"]), None)
            S.emit("vector", lambda e: e.tensor_mul(row(2), row(0), row(0)), (), None)
            S.emit("vector", lambda e: e.tensor_sub(row(1), row(1), row(2)), (), (V, 1))
            tk_var = (V, S.cnt[V])
            S.emit("scalar", lambda e: e.activation(
                row(2), row(1), AF.Ln, bias=ep5()), (tk_var, tk_ep5), (SC, 1))
            prev_bc["ln"] = (SC, S.cnt[SC])
            S.emit("scalar", lambda e: e.activation(
                row(3), row(2), AF.Exp, scale=-0.5),
                (prev_bc["ln"], prev_bc["r"], prev_bc["nmr"]), (SC, 1))
            tk_rstd = (SC, S.cnt[SC])
            ti["tk_rstd"] = tk_rstd
            S.emit("vector", lambda e: e.tensor_mul(row(0), row(0), row(3)),
                   (tk_rstd,), (V, 1))
            tk_nmr = (V, S.cnt[V])
            prev_bc["nmr"] = tk_nmr
            S.emit("tensor", lambda e, pbr=pbr: e.matmul(
                pbr[:, :], Ar[32:33, ST_ON: ST_ON + 128], row(3),
                start=True, stop=True), (tk_rstd, bfree[ti["nbr"]]), (T, 1))
            tk_bcr = (T, S.cnt[T])
            S.emit("tensor", lambda e, pbn=pbn: e.matmul(
                pbn[:, :], Ar[32:33, ST_ON: ST_ON + 128], row(0),
                start=True, stop=True), (tk_nmr, bfree[ti["nbn"]]), (T, 1))
            tk_bcn = (T, S.cnt[T])
            prev_bc["r"], prev_bc["n"] = tk_bcr, tk_bcn
            if not prefix:
                S.emit("vector", lambda e, tt=tt, pbr=pbr: e.tensor_copy(
                    Ar[:, BC1 + tt * 512: BC1 + tt * 512 + 512], pbr[:, :]),
                    (tk_bcr, TI[3]["tk_sq"]), (V, 1))
                S.emit("vector", lambda e, tt=tt, pbn=pbn: e.tensor_copy(
                    Ar[:, BC1 + 2048 + tt * 512: BC1 + 2048 + tt * 512 + 512],
                    pbn[:, :]), (tk_bcn,), (V, 1))
            par = j % 2
            t32 = Ar[0:32, RW + par * 512: RW + par * 512 + 512]
            S.emit("vector", lambda e, t32=t32, pst=pst: e.tensor_copy(
                t32, pst[0:32, :]), (ti["tk_lb"],) + tuple(tmp32_free[par]), (V, 1))
            tk_corr0 = (V, S.cnt[V])
            S.emit("vector", lambda e, t32=t32, pbr=pbr: e.tensor_mul(
                t32, t32, pbr[0:32, :]), (tk_bcr,), (V, 1))
            tk_corr1 = (V, S.cnt[V])
            S.emit("vector", lambda e, t32=t32, pbn=pbn: e.scalar_tensor_tensor(
                t32, pbn[0:32, :], Ar[0:32, ST_WSL: ST_WSL + 1], t32,
                ALU.mult, ALU.add), (tk_bcn, tk_wsl), (V, 1))
            tk_corr2 = (V, S.cnt[V])
            bfree[ti["nst"]] = tk_corr0
            bfree[ti["nst2"]] = tk_corr1
            bfree[ti["nbr"]] = tk_corr1
            bfree[ti["nbn"]] = tk_corr2
            ti["t32"] = t32
            ti["tk_corr2"] = tk_corr2
            if do_evac:
                a_tail_evac(j)

        def a_tail_evac(j):
            ti = TI[j]
            tt, prefix = ti["tt"], ti["prefix"]
            t32, tk_corr2 = ti["t32"], ti["tk_corr2"]
            par = j % 2
            w_scan1 = tuple(tk_scan1) if (tk_scan1 is not None and not prefix) else ()
            lam_tmp = Ar[0:16, GT2: GT2 + 512]
            tk_sig = S.emit("scalar", lambda e, t32=t32, lam_tmp=lam_tmp: e.activation(
                lam_tmp, t32[0:16, :], AF.Sigmoid, bias=Ar[0:16, ST_BL: ST_BL + 1]),
                (tk_corr2, tk_c["bl"], lamtmp_free[0]) + w_scan1, (SC, 1))
            for h_ in (0, 1):
                c_ = 2 * tt + h_
                dma("scalar", "QLB", Ar[16 * c_: 16 * c_ + 16, LAMC: LAMC + 256],
                    lam_tmp[:, h_ * 256: h_ * 256 + 256], (tk_sig,))
                dma("scalar", "QLB", Ar[16 * c_: 16 * c_ + 16, BXC: BXC + 256],
                    t32[16:32, h_ * 256: h_ * 256 + 256], (tk_corr2,) + w_scan1)
            tk_lb = ("QLB", S.cnt["QLB"])
            lamtmp_free[0] = tk_lb
            lam_toks.append(tk_lb)
            tmp32_free[par] = (tk_sig, tk_lb)

        if a >= 1 and a != 5:
            a_tail(a - 1)
        if a == 4:
            a_tail(4, do_evac=False)
            a_head(5)
            lam_b = Ar[:, LAMC: LAMC + 256]
            bx_b = Ar[:, BXC: BXC + 256]
            tk_h1 = S.emit("vector", lambda e, lam_b=lam_b, bx_b=bx_b:
                           e.tensor_tensor_scan(h_b, lam_b, bx_b, 0.0,
                                                ALU.mult, ALU.add),
                           tuple(lam_toks), (V, 1))
            tk_p1 = S.emit("vector", lambda e, lam_b=lam_b:
                           e.tensor_tensor_scan(P_b, lam_b, zero_b, 1.0,
                                                ALU.mult, ALU.add),
                           (tk_z0,), (V, 1))
            tk_f1 = S.emit("scalar", lambda e: e.copy(
                Ar[:, BXC + 1264: BXC + 1280], Ar[:, BXC + 1264: BXC + 1280]),
                (tk_h1,), (SC, 1))
            tk_f2 = S.emit("scalar", lambda e: e.copy(
                Ar[:, BXC + 1520: BXC + 1536], Ar[:, BXC + 1520: BXC + 1536]),
                (tk_p1,), (SC, 1))
            for c_ in range(8):
                dma("scalar", "QG1", hl8[:, c_: c_ + 1],
                    Ar[16 * c_: 16 * c_ + 16, BXC + 1279: BXC + 1280], (tk_f1,))
                dma("scalar", "QG1", pl8[:, c_: c_ + 1],
                    Ar[16 * c_: 16 * c_ + 16, BXC + 1535: BXC + 1536], (tk_f2,))
            tk_cb1 = S.emit("vector", lambda e: e.tensor_tensor_scan(
                compb, pl8, hl8, 0.0, ALU.mult, ALU.add),
                (("QG1", S.cnt["QG1"]),), (V, 1))
            tk_scan1 = (S.emit("scalar", lambda e: e.copy(
                Ar[0:16, ST_CAR: ST_CAR + 1], compb[:, 7:8]),
                (tk_cb1,), (SC, 1)),)
            lam_toks = []
            a_tail_evac(4)
        if a == 4:
            for oc_ in range(8):
                for kc_ in range(KC):
                    S.emit("tensor", lambda e, kc=kc_, oc=oc_: e.matmul(
                        pBC[:, oc: oc + 1],
                        Ar[:, WG + (kc * 8 + oc) * 128:
                           WG + (kc * 8 + oc) * 128 + 128],
                        ones_col, start=(kc == 0 and oc == 0),
                        stop=(kc == KC - 1), skip_group_check=True),
                        (tk_c["wg"], bfree["pBC"]) if oc_ == 0 and kc_ == 0
                        else (),
                        (T, 1) if kc_ == KC - 1 else None)
            tk_wgs = S.emit("vector", lambda e: e.tensor_copy(
                Ar[:, ST_WGS: ST_WGS + 8], pBC[:, 0:8]),
                ((T, S.cnt[T]),), (V, 1))
            bfree["pBC"] = tk_wgs
        for t in range(a * 8, a * 8 + 8):
            s = t % 4
            src_h_ = w1_h
            fc = t // 2
            h = t % 2
            inb = Ar[:, ZB + s * 1024: ZB + s * 1024 + 512]
            hib = Ar[:, ZB + s * 1024 + 512: ZB + s * 1024 + 768].bitcast(BF16)
            lob = Ar[:, ZB + s * 1024 + 768: ZB + s * 1024 + 1024].bitcast(BF16)
            tk_in = dma("sync", f"QW{s}",
                        inb, src_h_[fc * 128: fc * 128 + 128,
                                    h * 512: h * 512 + 512],
                        sp_cons[s])
            tk_hi = S.emit("gpsimd", lambda e, hib=hib, inb=inb: e.tensor_copy(
                hib, inb), (tk_in, sp_wout[s]), ("GC", 1))
            tk_lo = S.emit("gpsimd", lambda e, lob=lob, inb=inb, hib=hib:
                           e.tensor_tensor(lob, inb, hib, ALU.subtract),
                           (tk_hi,), ("GC", 1))
            sp_cons[s] = (tk_hi, tk_lo)
            sp_wb_pend.append((t, s, tk_lo))
            if len(sp_wb_pend) > 2:
                emit_split_wb(*sp_wb_pend.pop(0))
    for args in sp_wb_pend:
        emit_split_wb(*args)
    sp_wb_pend = []
    a_tail(7)

    last_pS_reader = bfree["pS"]
    last_pS2_reader = bfree["pS2"]
    last_pBC_reader = bfree["pBC"]
    last_pBC2_reader = bfree["pBC2"]

    gbank = [pA, pB]
    ybank = [pC, pD]
    g_free = [bfree["pA"], bfree["pB"]]
    y_free = [bfree["pC"], bfree["pD"]]
    z_free = {}
    gt_state = {"gt1": [None, None],
                "gt2": [("QLB", S.cnt["QLB"]), None], "zsq": None,
                "pBC2": bfree["pBC2"], "prevbc": (TI[7]["tk_rstd"],)}
    stat_free = {"pS": bfree["pS"], "pS2": bfree["pS2"], "pBC": bfree["pBC"]}
    r1_tok = {}
    CT = {}

    def zsl_ap(tt, oc):
        base = ZB if tt % 2 == 0 else LAMC
        return Ar[:, base + oc * 512: base + oc * 512 + 512]

    def ab_ap(tt, which):
        c0 = XPB + (tt % 2) * 1024 + (0 if which == "a" else 512)
        return Ar[:, c0: c0 + 512]

    def c_head_gates(tt):
        st_g = {}
        tk_gate_last = None
        for oc in range(8):
            par = (tt * 8 + oc) % 2
            for kc in range(KC):
                S.emit("tensor", lambda e, kc=kc, oc=oc, tt=tt, par=par: e.matmul(
                    gbank[par][:, :],
                    Ar[:, WG + (kc * 8 + oc) * 128: WG + (kc * 8 + oc) * 128 + 128],
                    xw_sl(kc, tt * ATILE, 512),
                    start=(kc == 0), stop=(kc == KC - 1)),
                    (tk_c["wg"], tk_c["xw"], g_free[par]) if kc == 0 else (),
                    (T, 1) if kc == KC - 1 else None)
            tk_gmm = (T, S.cnt[T])
            tk_gate_last = tk_gmm
            bcr = Ar[:, BC1 + tt * 512: BC1 + tt * 512 + 512]
            bcn = Ar[:, BC1 + 2048 + tt * 512: BC1 + 2048 + tt * 512 + 512]
            g1b_ = Ar[:, GT1: GT1 + 512] if par == 0 \
                else Ar[:, XPB + 2048: XPB + 2048 + 512]
            S.emit("vector", lambda e, par=par, bcr=bcr, g1b_=g1b_: e.tensor_mul(
                g1b_, gbank[par][:, :], bcr),
                (tk_gmm, gt_state["gt1"][par]), (V, 1))
            tk_g1 = (V, S.cnt[V])
            g_free[par] = tk_g1
            S.emit("vector", lambda e, oc=oc, bcn=bcn, g1b_=g1b_: e.scalar_tensor_tensor(
                g1b_, bcn,
                Ar[:, ST_WGS + oc: ST_WGS + oc + 1], g1b_,
                ALU.mult, ALU.add), (tk_wgs,), (V, 1))
            tk_g1b = (V, S.cnt[V])
            w_split = tuple((f"GW{s_}", S.cnt.get(f"GW{s_}", 0))
                            for s_ in range(4)) if tt == 0 else ()
            w_scr = (tk_scan2,) if tt == 1 else ()
            S.emit("scalar", lambda e, g1b_=g1b_, tt=tt, oc=oc: e.activation(
                zsl_ap(tt, oc), g1b_, AF.Sigmoid),
                (tk_g1b, z_free.get((tt - 2, oc))) + w_split + w_scr, (SC, 1))
            tk_g2 = (SC, S.cnt[SC])
            gt_state["gt1"][par] = tk_g2
            st_g[oc] = (tk_g2, par)
        CT[tt] = dict(tk_gate_last=tk_gate_last, st_g=st_g)

    def c_head_wz(tt):
        ct = CT[tt]
        tk_g = {}
        for oc in range(8):
            tk_g2, par = ct["st_g"][oc]
            S.emit("tensor", lambda e, oc=oc, tt=tt, par=par: e.matmul(
                ybank[par][:, :],
                Ar[0:16, ST_WC + oc * 128: ST_WC + oc * 128 + 128],
                Ar[0:16, HSC + tt * 512: HSC + tt * 512 + 512],
                start=True, stop=True),
                (tk_scan2, tk_c["wc"], y_free[par]), (T, 1))
            tk_ymm = (T, S.cnt[T])
            S.emit("vector", lambda e, par=par, tt=tt, oc=oc: e.tensor_mul(
                zsl_ap(tt, oc), ybank[par][:, :], zsl_ap(tt, oc)),
                (tk_ymm, tk_g2), (V, 1))
            tk_z = (V, S.cnt[V])
            y_free[par] = tk_z
            gt_state["gt2"][par] = tk_z
            tk_g[oc] = tk_z
        ct["tk_g"] = tk_g

    def c_head(tt):
        c_head_gates(tt)
        c_head_wz(tt)

    def c_stats(tt):
        tk_g = CT[tt]["tk_g"]
        for oc in range(8):
            sq = Ar[:, OB: OB + 512]
            tk_sq_w = S.emit("scalar", lambda e, tt=tt, oc=oc, sq=sq: e.activation(
                sq, zsl_ap(tt, oc), AF.Square), (tk_g[oc], gt_state["zsq"]), (SC, 1))
            S.emit("tensor", lambda e, sq=sq, oc=oc: e.matmul(
                pS2[0:33, :], Ar[:, ST_ZO: ST_ZO + 33], sq,
                start=(oc == 0), stop=(oc == 7)),
                (tk_sq_w,) + ((stat_free["pS2"],) if oc == 0 else ()), (T, 1))
            tk_sq2 = S.emit("tensor", lambda e, sq=sq, oc=oc: e.matmul(
                pBC[0:33, :], Ar[:, ST_WSQ + oc * 33: ST_WSQ + oc * 33 + 33], sq,
                start=(oc == 0), stop=(oc == 7)),
                (tk_wsq,) + ((stat_free["pBC"],) if oc == 0 else ()), (T, 1))
            gt_state["zsq"] = tk_sq2
            S.emit("tensor", lambda e, tt=tt, oc=oc: e.matmul(
                pS[0:33, :], Ar[:, ST_WS + oc * 33: ST_WS + oc * 33 + 33],
                zsl_ap(tt, oc), start=(oc == 0), stop=(oc == 7)),
                (tk_g[oc], tk_c["ws"]) + ((stat_free["pS"],) if oc == 0 else ()),
                (T, 1))
        CT[tt]["tk_stats"] = (T, S.cnt[T])

    ct_scal = {"t": ()}

    def c_tail(tt):
        ct = CT[tt]
        S.emit("vector", lambda e: e.tensor_single_scalar(
            row(0), pS2[32:33, :], 1.0 / D, ALU.mult),
            (ct["tk_stats"],) + gt_state["prevbc"], (V, 1))
        S.emit("scalar", lambda e: e.activation(
            row(1), row(0), AF.Ln, bias=ep6()), ((V, S.cnt[V]), tk_ep6), (SC, 1))
        tk_ln1 = (SC, S.cnt[SC])
        S.emit("scalar", lambda e: e.activation(
            row(2), row(1), AF.Exp, scale=-0.5), (tk_ln1,), (SC, 1))
        tk_rms = (SC, S.cnt[SC])
        S.emit("vector", lambda e: e.tensor_single_scalar(
            row(0), pS[32:33, :], 1.0 / D, ALU.mult), (tk_ln1,), None)
        S.emit("vector", lambda e: e.tensor_mul(row(1), row(0), row(2)),
               (tk_rms,), None)
        S.emit("vector", lambda e: e.tensor_single_scalar(
            row(0), pBC[32:33, :], 1.0 / D, ALU.mult), (), (V, 1))
        tk_rowmid = (V, S.cnt[V])
        stat_free["pS"] = tk_rowmid
        stat_free["pS2"] = tk_rowmid
        stat_free["pBC"] = tk_rowmid
        S.emit("vector", lambda e: e.tensor_mul(row(0), row(0), row(2)), (), None)
        S.emit("vector", lambda e: e.tensor_mul(row(0), row(0), row(2)), (), None)
        S.emit("vector", lambda e: e.tensor_mul(row(3), row(1), row(1)), (), None)
        S.emit("vector", lambda e: e.tensor_sub(row(0), row(0), row(3)), (), (V, 1))
        S.emit("scalar", lambda e: e.activation(
            row(3), row(0), AF.Ln, bias=ep5()), ((V, S.cnt[V]), tk_ep5), (SC, 1))
        S.emit("scalar", lambda e: e.activation(
            row(0), row(3), AF.Exp, scale=-0.5), ((SC, S.cnt[SC]),), (SC, 1))
        tk_rstd2 = (SC, S.cnt[SC])
        S.emit("vector", lambda e: e.tensor_mul(row(3), row(2), row(0)),
               (tk_rstd2,), (V, 1))
        tk_alpha = (V, S.cnt[V])
        ct_scal["t"] = (tk_rms, tk_rstd2)
        S.emit("vector", lambda e: e.tensor_mul(row(1), row(1), row(0)), (), (V, 1))
        tk_beta = (V, S.cnt[V])
        S.emit("tensor", lambda e: e.matmul(
            pBC2[:, :], Ar[32:33, ST_ON: ST_ON + 128], row(3), start=True,
            stop=True), (tk_alpha, gt_state["pBC2"]), (T, 1))
        tk_abc = (T, S.cnt[T])
        S.emit("tensor", lambda e: e.matmul(
            pS[:, :], Ar[32:33, ST_ON: ST_ON + 128], row(1), start=True,
            stop=True), (tk_beta, stat_free["pS"]), (T, 1))
        tk_bbc = (T, S.cnt[T])
        S.emit("vector", lambda e, tt=tt: e.tensor_copy(ab_ap(tt, "a"), pBC2[:, :]),
               (tk_abc,), (V, 1))
        tk_acp = (V, S.cnt[V])
        S.emit("vector", lambda e, tt=tt: e.tensor_copy(ab_ap(tt, "b"), pS[:, :]),
               (tk_bbc,), (V, 1))
        tk_bcp = (V, S.cnt[V])
        gt_state["pBC2"] = tk_acp
        stat_free["pS"] = tk_bcp
        gt_state["prevbc"] = (tk_bbc,)
        for h_ in (0, 1):
            for oc in range(8):
                g1h = Ar[:, GT1 + h_ * 256: GT1 + h_ * 256 + 256]
                g2h = Ar[:, GT2 + h_ * 256: GT2 + h_ * 256 + 256]
                zh = zsl_ap(tt, oc)[:, h_ * 256: h_ * 256 + 256]
                aah = ab_ap(tt, "a")[:, h_ * 256: h_ * 256 + 256]
                abh = ab_ap(tt, "b")[:, h_ * 256: h_ * 256 + 256]
                S.emit("vector", lambda e, zh=zh, aah=aah, g1h=g1h: e.tensor_mul(
                    g1h, zh, aah),
                    (tk_acp, gt_state["gt1"][0]), (V, 1))
                tk_t1 = (V, S.cnt[V])
                S.emit("vector", lambda e, oc=oc, g1h=g1h, g2h=g2h, abh=abh:
                       e.scalar_tensor_tensor(
                           g2h, g1h,
                           Ar[:, ST_RWT + oc: ST_RWT + oc + 1], abh,
                           ALU.mult, ALU.subtract),
                       (tk_bcp, tk_c["rw"], gt_state["gt2"][0]), (V, 1))
                tk_t2 = (V, S.cnt[V])
                gt_state["gt1"][0] = tk_t2
                xsl = xw_sl(oc, tt * ATILE + h_ * 256, 256)
                S.emit("vector", lambda e, xsl=xsl, g2h=g2h: e.tensor_add(
                    xsl, g2h, xsl), (ct["tk_gate_last"],), (V, 1))
                tk_r1o = (V, S.cnt[V])
                gt_state["gt2"][0] = tk_r1o
                r1_tok[(oc, tt, h_)] = tk_r1o
                if h_ == 1:
                    z_free[(tt, oc)] = tk_r1o

    lam_b = Ar[:, LAMC: LAMC + 256]
    bx_b = Ar[:, BXC: BXC + 256]
    tk_h2 = S.emit("vector", lambda e, lam_b=lam_b, bx_b=bx_b:
                   e.tensor_tensor_scan(h_b, lam_b, bx_b, 0.0,
                                        ALU.mult, ALU.add),
                   tuple(lam_toks) + tuple(tk_scan1) + (("QG1", 256),), (V, 1))
    tk_p2 = S.emit("vector", lambda e, lam_b=lam_b:
                   e.tensor_tensor_scan(P_b, lam_b, zero_b, 1.0,
                                        ALU.mult, ALU.add),
                   (("QG1", 256),), (V, 1))
    tk_f3 = S.emit("scalar", lambda e: e.copy(
        Ar[:, BXC + 1264: BXC + 1280], Ar[:, BXC + 1264: BXC + 1280]),
        (tk_h2,), (SC, 1))
    tk_f4 = S.emit("scalar", lambda e: e.copy(
        Ar[:, BXC + 1520: BXC + 1536], Ar[:, BXC + 1520: BXC + 1536]),
        (tk_p2,), (SC, 1))
    for c_ in range(8):
        dma("scalar", "QG1", hl8[:, c_: c_ + 1],
            Ar[16 * c_: 16 * c_ + 16, BXC + 1279: BXC + 1280],
            (tk_f3,) + tuple(tk_scan1))
        dma("scalar", "QG1", pl8[:, c_: c_ + 1],
            Ar[16 * c_: 16 * c_ + 16, BXC + 1535: BXC + 1536],
            (tk_f4,) + tuple(tk_scan1))
    tk_cb2 = S.emit("vector", lambda e: e.tensor_tensor_scan(
        compb, pl8, hl8, Ar[0:16, ST_CAR: ST_CAR + 1], ALU.mult, ALU.add),
        (("QG1", S.cnt["QG1"]),) + tuple(tk_scan1), (V, 1))
    tk_fc = S.emit("scalar", lambda e: e.copy(compb, compb), (tk_cb2,), (SC, 1))
    tk_c0 = S.emit("scalar", lambda e: e.copy(
        carr_bc[0:16, :], Ar[0:16, ST_CAR: ST_CAR + 1]),
        tuple(tk_scan1), (SC, 1))
    for c_ in range(1, 8):
        dma("scalar", "QG2", carr_bc[16 * c_: 16 * c_ + 16, :],
            compb[:, c_ - 1: c_], (tk_fc,))
    tk_fix = S.emit("vector", lambda e: e.scalar_tensor_tensor(
        h_b, P_b, carr_bc, h_b, ALU.mult, ALU.add),
        (("QG2", S.cnt["QG2"]), tk_c0, tk_f3, tk_f4), (V, 1))
    for c_ in range(8):
        dma("scalar", "QU", Ar[0:16, HSC + c_ * 256: HSC + c_ * 256 + 256],
            Ar[16 * c_: 16 * c_ + 16, BXC + 1024: BXC + 1280], (tk_fix,))
    tk_scan2 = ("QU", S.cnt["QU"])
    c_head_gates(0)
    c_head_wz(0)
    c_stats(0)
    for tt in range(1, NAT):
        c_head(tt)
        c_tail(tt - 1)
        c_stats(tt)
    c_tail(NAT - 1)

    if "r1" in dict(dbg):
        S.emit("gpsimd", lambda e: e.dma_start(
            out=dbg_h["r1"], in_=Ar[:, XW: XW + KC * TW]),
            tuple(r1_tok.values()), ("GD", 16))
    if "hs" in dict(dbg):
        S.emit("gpsimd", lambda e: e.dma_start(
            out=dbg_h["hs"], in_=Ar[0:16, HSC: HSC + 2048]), (tk_scan2,), ("GD", 16))

    hbank = [pA, pB]
    h_free = [(g_free[0],), (g_free[1],)]
    w1b_free = [z_free[(2, 1)], z_free[(2, 3)], z_free[(3, 7)]]
    w2b_free = [z_free[(2, 5)], z_free[(2, 7)], z_free[(3, 7)]]
    W1B = [ZB, ZB + 1024, HSC]
    W2B = [ZB + 2048, ZB + 3072, HSC + 1024]
    y2bank = [pC, pD, pBC, pBC2]
    y2_free = [y_free[0], y_free[1], stat_free["pBC"], gt_state["pBC2"]]
    sqD_free = [gt_state["zsq"], gt_state["zsq"]]
    DT = {}
    lastz = tuple(z_free[(3, oc)] for oc in range(8))

    def y2s_ap(oc):
        return Ar[:, LAMC + oc * 256: LAMC + oc * 256 + 256]

    def xfh_ap(st, kc):
        b = Ar[:, XPB + (st % 2) * 2048: XPB + (st % 2) * 2048 + 1024].bitcast(BF16)
        return b[:, kc * 256: kc * 256 + 256]

    def xfl_ap(st, kc):
        b = Ar[:, XPB + (st % 2) * 2048 + 1024: XPB + (st % 2) * 2048 + 2048].bitcast(BF16)
        return b[:, kc * 256: kc * 256 + 256]

    def row3(i):
        return Ar[32:33, RW + i * 512: RW + i * 512 + 256]

    def row4(i):
        return Ar[32:33, RW + i * 512 + 256: RW + i * 512 + 512]

    n = DTILE

    ln3_state = {"ln": None}

    def d_ln3(st):
        tt = st // 2
        def r1_sl(kc, off=st * DTILE):
            return xw_sl(kc, off, DTILE)
        for kc in range(KC):
            S.emit("tensor", lambda e, kc=kc, r1_sl=r1_sl: e.matmul(
                pS[0:33, 0:n], Ar[:, ST_WR + kc * 33: ST_WR + kc * 33 + 33],
                r1_sl(kc), start=(kc == 0), stop=(kc == KC - 1)),
                (tk_c["wr"], r1_tok[(kc, tt, st % 2)]) +
                ((DT[st]["pS_free"],) if kc == 0 else ()),
                (T, 1) if kc == KC - 1 else None)
        tk_wr = (T, S.cnt[T])
        for kc in range(KC):
            par = kc % 2
            sq = Ar[:, OB + par * 256: OB + par * 256 + 256]
            tk_sq_w = S.emit("scalar", lambda e, sq=sq, kc=kc, r1_sl=r1_sl: e.activation(
                sq, r1_sl(kc), AF.Square),
                (r1_tok[(kc, tt, st % 2)], sqD_free[par]), (SC, 1))
            tk_p = S.emit("tensor", lambda e, sq=sq, kc=kc: e.matmul(
                pS2[0:33, 0:n], Ar[:, ST_ZO: ST_ZO + 33], sq,
                start=(kc == 0), stop=(kc == KC - 1)),
                (tk_sq_w,) + ((DT[st]["pS2_free"],) if kc == 0 else ()), (T, 1))
            sqD_free[par] = tk_p
        tk_sqg = (T, S.cnt[T])
        lg = Ar[0:8, RW + (st % 2) * 512: RW + (st % 2) * 512 + n]
        S.emit("vector", lambda e: e.tensor_single_scalar(
            row3(0), pS[32:33, 0:n], -1.0 / D, ALU.mult),
            (tk_wr,) + DT[st]["rowguard3"], None)
        S.emit("vector", lambda e, lg=lg: e.tensor_copy(lg, pS[0:8, 0:n]),
               (tk_wr,) + DT[st]["lgguard"], None)
        S.emit("vector", lambda e: e.tensor_single_scalar(
            row3(1), pS2[32:33, 0:n], 1.0 / D, ALU.mult),
            (tk_sqg, ln3_state["ln"]), None)
        S.emit("vector", lambda e: e.tensor_mul(row3(2), row3(0), row3(0)), (), None)
        S.emit("vector", lambda e: e.tensor_sub(row3(1), row3(1), row3(2)), (), (V, 1))
        S.emit("scalar", lambda e: e.activation(
            row3(2), row3(1), AF.Ln, bias=ep5()), ((V, S.cnt[V]), tk_ep5), (SC, 1))
        ln3_state["ln"] = (SC, S.cnt[SC])
        S.emit("scalar", lambda e: e.activation(
            row3(3), row3(2), AF.Exp, scale=-0.5), (ln3_state["ln"],), (SC, 1))
        tk_rstd3 = (SC, S.cnt[SC])
        S.emit("vector", lambda e: e.tensor_mul(row3(0), row3(0), row3(3)),
               (tk_rstd3,), (V, 1))
        tk_nmr3 = (V, S.cnt[V])
        DT[st].update(tk_wr=tk_wr, tk_rstd3=tk_rstd3, tk_nmr3=tk_nmr3, lg=lg,
                      rowread3=(V, S.cnt[V]))

    def d_bc3_smax_xf(st):
        d = DT[st]
        tt = st // 2
        def r1_sl(kc, off=st * DTILE):
            return xw_sl(kc, off, DTILE)
        lg = d["lg"]
        S.emit("tensor", lambda e: e.matmul(
            pS2[:, 0:n], Ar[32:33, ST_ON: ST_ON + 128], row3(3),
            start=True, stop=True), (d["tk_rstd3"], d["rowread3"]), (T, 1))
        tk_bc3r = (T, S.cnt[T])
        S.emit("tensor", lambda e: e.matmul(
            pS[:, 0:n], Ar[32:33, ST_ON: ST_ON + 128], row3(0),
            start=True, stop=True), (d["tk_nmr3"],), (T, 1))
        tk_bc3n = (T, S.cnt[T])
        S.emit("vector", lambda e, lg=lg: e.tensor_mul(
            lg, lg, pS2[0:8, 0:n]), (tk_bc3r,), (V, 1))
        S.emit("vector", lambda e, lg=lg: e.scalar_tensor_tensor(
            lg, pS[0:8, 0:n], Ar[0:8, ST_WSR: ST_WSR + 1], lg,
            ALU.mult, ALU.add), (tk_wsr, tk_bc3n), (V, 1))
        tk_lg = (V, S.cnt[V])
        tk_xf = {}
        xtmp = Ar[:, BXC + 512: BXC + 512 + 256]
        for kc in range(KC):
            S.emit("vector", lambda e, kc=kc, xtmp=xtmp, r1_sl=r1_sl: e.tensor_mul(
                xtmp, r1_sl(kc), pS2[:, 0:n]),
                (tk_bc3r,) + d["xfguard"], (V, 1))
            S.emit("vector", lambda e, xtmp=xtmp: e.tensor_add(
                xtmp, xtmp, pS[:, 0:n]), (tk_bc3n,), (V, 1))
            S.emit("vector", lambda e, kc=kc, st=st, xtmp=xtmp: e.tensor_copy(
                xfh_ap(st, kc), xtmp), (), (V, 1))
            S.emit("vector", lambda e, kc=kc, st=st, xtmp=xtmp: e.tensor_tensor(
                xfl_ap(st, kc), xtmp, xfh_ap(st, kc), ALU.subtract), (), (V, 1))
            tk_xf[kc] = (V, S.cnt[V])
        eg = Ar[0:8, RW + 1024 + (st % 2) * 256: RW + 1024 + (st % 2) * 256 + n]
        S.emit("scalar", lambda e, lg=lg, eg=eg: e.activation(
            eg, lg, AF.Exp), (tk_lg,), (SC, 1))
        tk_eg = (SC, S.cnt[SC])
        S.emit("tensor", lambda e, eg=eg: e.matmul(
            pB[0:33, 0:n], Ar[0:8, ST_ZO: ST_ZO + 33], eg,
            start=True, stop=True),
            (tk_eg,) + tuple(h_free[1]), (T, 1))
        S.emit("tensor", lambda e, eg=eg: e.matmul(
            pA[0:33, 0:n], Ar[0:8, ST_ZSEL: ST_ZSEL + 33], eg,
            start=True, stop=True),
            (tk_eg, tk_c["zs"]) + tuple(h_free[0]), (T, 1))
        tk_se = (T, S.cnt[T])
        sum4 = Ar[32:33, SUM4C + (st % 2) * 256: SUM4C + (st % 2) * 256 + n]
        num4 = Ar[32:33, NUM4C + (st % 2) * 256: NUM4C + (st % 2) * 256 + n]
        w_s4 = (DT[st - 2]["tk_lnp"],) if st >= 2 and "tk_lnp" in DT.get(st - 2, {}) \
            else ()
        S.emit("vector", lambda e, sum4=sum4: e.tensor_copy(sum4, pB[32:33, 0:n]),
               ((T, S.cnt[T] - 1),) + w_s4, (V, 1))
        S.emit("vector", lambda e, num4=num4: e.tensor_copy(num4, pA[32:33, 0:n]),
               (tk_se,), (V, 1))
        tk_sec = (V, S.cnt[V])
        d.update(tk_xf=tk_xf, tk_sec=tk_sec, sum4=sum4, num4=num4)
        h_free[0] = (tk_sec,)
        h_free[1] = (tk_sec,)
        DT[st + 1] = dict(pS_free=tk_xf[KC - 1], pS2_free=tk_xf[KC - 1],
                          rowguard3=(tk_bc3r, tk_bc3n), lgguard=(tk_lg,),
                          xfguard=(), p7guard=())

    def d_W1(st):
        d = DT[st]
        for fc in range(FC):
            par = fc % 3
            parh = fc % 2
            w1b = Ar[:, W1B[par]: W1B[par] + 1024].bitcast(BF16)
            tk_w1l = dma("sync", ("Q1E", "Q1O", "Q1T")[par],
                         w1b, w1s_h[fc * 128: fc * 128 + 128, :],
                         (w1b_free[par],))
            for kc in range(KC):
                for p3 in range(3):
                    b0 = (kc // 4) * 1024 + (0 if p3 < 2 else 512) \
                        + (kc % 4) * 128
                    wsl = w1b[:, b0: b0 + 128]
                    xsl = (xfh_ap(st, kc) if p3 != 1 else xfl_ap(st, kc))
                    S.emit("tensor", lambda e, wsl=wsl, xsl=xsl, par=parh,
                           kc=kc, p3=p3: e.matmul(
                        hbank[par][:, 0:n], wsl, xsl,
                        start=(kc == 0 and p3 == 0),
                        stop=(kc == KC - 1 and p3 == 2)),
                        ((tk_w1l,) + tuple(h_free[parh]) + tuple(d["tk_xf"].values())
                         if kc == 0 and p3 == 0 else ()),
                        (T, 1) if kc == KC - 1 and p3 == 2 else None)
            tk_h = (T, S.cnt[T])
            w1b_free[par] = tk_h
            tk_gelu = S.emit("scalar", lambda e, fc=fc, par=parh: e.activation(
                Ar[:, WG + fc * 256: WG + fc * 256 + 256], hbank[par][:, 0:n],
                AF.Gelu, bias=Ar[:, ST_B1 + fc: ST_B1 + fc + 1]),
                (tk_h, tk_c["b1"], d["h_cons"]), (SC, 1))
            h_free[parh] = (tk_gelu,)
            d.setdefault("gelu", {})[fc] = tk_gelu

    def d_W2(st):
        d = DT[st]
        for fc in range(FC):
            par = fc % 3
            w2b = Ar[:, W2B[par]: W2B[par] + 1024]
            tk_w2l = dma("gpsimd", ("Q2E", "Q2O", "Q2T")[par],
                         w2b, w2_h[fc * 128: fc * 128 + 128, :],
                         (w2b_free[par],))
            for oc in range(8):
                bank = y2bank[oc // 2]
                half = (oc % 2) * 256
                wts = (tk_w2l, d["gelu"][fc]) if oc == 0 else ()
                if fc == 0 and oc % 2 == 0:
                    wts = wts + (y2_free[oc // 2],)
                    if bank in (pBC, pBC2):
                        wts = wts + tuple(d["tk_xf"].values())
                S.emit("tensor",
                       lambda e, w2b=w2b, bank=bank, half=half,
                       fc=fc, oc=oc: e.matmul(
                           bank[:, half: half + n],
                           w2b[:, oc * 128: oc * 128 + 128],
                           Ar[:, WG + fc * 256: WG + fc * 256 + 256],
                           start=(fc == 0 and oc % 2 == 0),
                           stop=(fc == FC - 1),
                           skip_group_check=True),
                       wts, (T, 1) if oc == 7 else None)
            tk_w2fc = (T, S.cnt[T])
            w2b_free[par] = tk_w2fc
        d["tk_w2last"] = tk_w2fc

    def d_LN4(st):
        d = DT[st]
        tk_y2 = {}
        for oc in range(8):
            bank = y2bank[oc // 2]
            half = (oc % 2) * 256
            S.emit("scalar", lambda e, bank=bank, half=half, oc=oc:
                   e.activation(
                       y2s_ap(oc), bank[:, half: half + n], AF.Identity,
                       bias=Ar[:, ST_B2 + oc: ST_B2 + oc + 1]),
                   (d["tk_w2last"], tk_c["b2"]) + d["y2guard"],
                   (SC, 1))
            tk_y2[oc] = (SC, S.cnt[SC])
        for oc in range(8):
            par = oc % 2
            S.emit("tensor", lambda e, oc=oc: e.matmul(
                pC[0:33, 0:n], Ar[:, ST_ZO: ST_ZO + 33], y2s_ap(oc),
                start=(oc == 0), stop=(oc == 7)),
                (tk_y2[oc],) + ((tk_y2[1],) if oc == 0 else ()), (T, 1))
            sq = Ar[:, OB + par * 256: OB + par * 256 + 256]
            tk_sq_w = S.emit("scalar", lambda e, oc=oc, sq=sq: e.activation(
                sq, y2s_ap(oc), AF.Square), (tk_y2[oc], sqD_free[par]), (SC, 1))
            tk_p = S.emit("tensor", lambda e, sq=sq, oc=oc: e.matmul(
                pD[0:33, 0:n], Ar[:, ST_ZO: ST_ZO + 33], sq,
                start=(oc == 0), stop=(oc == 7)),
                (tk_sq_w,) + ((tk_y2[3],) if oc == 0 else ()), (T, 1))
            sqD_free[par] = tk_p
        tk_s4 = (T, S.cnt[T])
        sum4 = d["sum4"]
        num4 = d["num4"]
        S.emit("vector", lambda e: e.tensor_single_scalar(
            row4(0), pC[32:33, 0:n], -1.0 / D, ALU.mult),
            (tk_s4,) + d.get("rowguard4", ()), None)
        S.emit("vector", lambda e: e.tensor_single_scalar(
            row4(1), pD[32:33, 0:n], 1.0 / D, ALU.mult), (), None)
        S.emit("vector", lambda e: e.tensor_mul(row4(2), row4(0), row4(0)), (), None)
        S.emit("vector", lambda e: e.tensor_sub(row4(1), row4(1), row4(2)), (), (V, 1))
        tk_mu2r = (V, S.cnt[V])
        S.emit("scalar", lambda e, sum4=sum4: e.activation(
            row4(2), sum4, AF.Ln), (d["tk_sec"], tk_mu2r), (SC, 1))
        d["tk_lnp"] = (SC, S.cnt[SC])
        S.emit("scalar", lambda e: e.activation(
            row4(2), row4(2), AF.Exp, scale=-1.0), (d["tk_lnp"],), (SC, 1))
        tk_p7 = (SC, S.cnt[SC])
        S.emit("vector", lambda e, num4=num4: e.tensor_mul(row4(2), num4, row4(2)),
               (tk_p7,), None)
        S.emit("vector", lambda e: e.tensor_mul(row4(1), row4(1), row4(2)), (), None)
        S.emit("vector", lambda e: e.tensor_mul(row4(1), row4(1), row4(2)), (), (V, 1))
        tk_var4 = (V, S.cnt[V])
        S.emit("scalar", lambda e: e.activation(
            row4(3), row4(1), AF.Ln, bias=ep5()), (tk_var4, tk_ep5), (SC, 1))
        S.emit("scalar", lambda e: e.activation(
            row4(3), row4(3), AF.Exp, scale=-0.5), ((SC, S.cnt[SC]),), (SC, 1))
        tk_rsd4 = (SC, S.cnt[SC])
        S.emit("vector", lambda e: e.tensor_mul(row4(3), row4(3), row4(2)),
               (tk_rsd4,), (V, 1))
        tk_rstd4 = (V, S.cnt[V])
        S.emit("vector", lambda e: e.tensor_mul(row4(0), row4(0), row4(3)), (), (V, 1))
        tk_nmr4 = (V, S.cnt[V])
        d.update(tk_y2=tk_y2, tk_rstd4=tk_rstd4, tk_nmr4=tk_nmr4,
                 rowread4=(V, S.cnt[V]))

    def d_out(st):
        d = DT[st]
        off = st * DTILE
        S.emit("tensor", lambda e: e.matmul(
            pBC[:, 0:n], Ar[32:33, ST_ON: ST_ON + 128], row4(3),
            start=True, stop=True),
            (d["tk_rstd4"], d["rowread4"], d["tk_y2"][5]), (T, 1))
        tk_bc4r = (T, S.cnt[T])
        S.emit("tensor", lambda e: e.matmul(
            pBC2[:, 0:n], Ar[32:33, ST_ON: ST_ON + 128], row4(0),
            start=True, stop=True), (d["tk_nmr4"], d["tk_y2"][7]), (T, 1))
        tk_bc4n = (T, S.cnt[T])
        obp_free = d["obp_free"]
        tk_rowfree = None
        for oc in range(8):
            par = oc % 2
            ot = Ar[:, GT2 + par * 256: GT2 + par * 256 + 256]
            S.emit("vector", lambda e, ot=ot, oc=oc: e.tensor_mul(
                ot, y2s_ap(oc), pBC[:, 0:n]), (tk_bc4r, obp_free[par]), (V, 1))
            S.emit("vector", lambda e, ot=ot: e.tensor_add(
                ot, ot, pBC2[:, 0:n]), (tk_bc4n,), None)
            S.emit("vector", lambda e, ot=ot, oc=oc, off=off: e.tensor_add(
                ot, ot, xw_sl(oc, off, DTILE)), (), (V, 1))
            tk_o = (V, S.cnt[V])
            tk_rowfree = tk_o
            tk_st = S.emit("sync", lambda e, ot=ot, oc=oc, off=off: e.dma_start(
                out=out_h[:, oc * TW + off: oc * TW + off + DTILE], in_=ot),
                (tk_o,), ("GOE" if par == 0 else "GOO", 16))
            obp_free[par] = tk_st
            out_toks.append(tk_st)
        d["tk_out"] = tk_rowfree
        return dict(pBC=tk_rowfree, pBC2=tk_rowfree,
                    pC=d["rowread4"], pD=d["rowread4"])

    out_toks = []
    if skip_d:
        S.emit("gpsimd", lambda e: e.engine_nop(),
               tuple(r1_tok.values()), None)
    DT[0] = dict(pS_free=stat_free["pS"], pS2_free=stat_free["pS2"],
                 rowguard3=ct_scal["t"], rowguard4=ct_scal["t"],
                 lgguard=(), xfguard=(), p7guard=())
    DT[0]["h_cons"] = None
    DT[0]["y2guard"] = lastz
    DT[0]["obp_free"] = [gt_state["gt2"][0], gt_state["gt2"][0]]
    if not skip_d:
        d_ln3(0)
        d_bc3_smax_xf(0)
    for st in (range(NDT) if not skip_d else ()):
        d_W1(st)
        if st >= 1:
            bank_toks = d_out(st - 1)
            y2_free = [bank_toks["pC"], bank_toks["pD"],
                       bank_toks["pBC"], bank_toks["pBC2"]]
        if st < NDT - 1:
            DT[st + 1]["h_cons"] = None
            d_ln3(st + 1)
        d_W2(st)
        if st < NDT - 1:
            d_bc3_smax_xf(st + 1)
            DT[st + 2]["xfguard"] = ((T, S.cnt[T]),)
            DT[st + 2]["p7guard"] = ()
        d_LN4(st)
        if st < NDT - 1:
            DT[st + 1]["h_cons"] = DT[st]["tk_w2last"]
            DT[st + 1]["y2guard"] = tuple(DT[st]["tk_y2"].values()) + (
                (DT[st]["tk_out"],) if "tk_out" in DT[st] else ())
            DT[st + 1]["obp_free"] = DT[st]["obp_free"]
    if not skip_d:
        d_out(NDT - 1)

    S.emit("gpsimd", lambda e: e.engine_nop(),
           (("GOE", S.cnt.get("GOE", 0)), ("GOO", S.cnt.get("GOO", 0))), None)

    sems = {}
    sem_names = ["T", "V", "SC", "QC", "QP", "Q1E", "Q1O", "Q2E", "Q2O",
                 "GOE", "GOO", "GD", "QW0", "QW1", "QW2", "QW3",
                 "GW0", "GW1", "GW2", "GW3", "GC", "Q1T", "Q2T",
                 "QLB", "QG1", "QG2", "QU", "QX", "QG"]
    for snm in sem_names:
        sems[snm] = nc.alloc_semaphore("sem" + snm)
    with (
        nc.Block() as block,
    ):

        def replay(name):
            def body(eng):
                last = {}
                for fn, waits, inc in S.ops[name]:
                    for (s, v) in waits:
                        if last.get(s, 0) < v:
                            eng.wait_ge(sems[s], v)
                            last[s] = v
                    inst = fn(eng)
                    if inc is not None:
                        inst.then_inc(sems[inc[0]], inc[1])
            return body

        block.sync(replay("sync"))
        block.gpsimd(replay("gpsimd"))
        block.scalar(replay("scalar"))
        block.vector(replay("vector"))
        block.tensor(replay("tensor"))
    return nc


def make_in_maps(inputs):
    x = np.asarray(inputs["L0_attn_x"], np.float32)
    Wl = np.asarray(inputs["L0_attn_W_lambda"], np.float32)
    bl = np.asarray(inputs["L0_attn_b_lambda"], np.float32)
    Wb = np.asarray(inputs["L0_attn_W_B"], np.float32)
    Wc = np.asarray(inputs["L0_attn_W_C"], np.float32)
    Wgt = np.asarray(inputs["L0_attn_W_gate"], np.float32)
    rw = np.asarray(inputs["L0_attn_rms_weight"], np.float32)
    Wr = np.asarray(inputs["L0_ffn_Wr"], np.float32)
    W1 = np.asarray(inputs["L0_ffn_W1"], np.float32)
    b1 = np.asarray(inputs["L0_ffn_b1"], np.float32)
    W2 = np.asarray(inputs["L0_ffn_W2"], np.float32)
    b2 = np.asarray(inputs["L0_ffn_b2"], np.float32)

    ones8 = np.ones((8, 128, 1), np.float32)
    wlb = np.ascontiguousarray(np.concatenate(
        [Wl.T.reshape(8, 128, 16), Wb.T.reshape(8, 128, 16), ones8],
        axis=2).transpose(1, 0, 2).reshape(128, 264))
    perm = [7, 0, 1, 2, 3, 4, 5, 6]
    wr33 = np.zeros((8, 128, 33), np.float32)
    wr33[:, :, 0:8] = Wr[perm].T.reshape(8, 128, 8)
    wr33[:, :, 32] = 1.0
    wr33 = np.ascontiguousarray(wr33.transpose(1, 0, 2).reshape(128, 264))
    ws33 = np.zeros((8, 128, 33), np.float32)
    ws33[:, :, 32] = rw.reshape(8, 128)
    ws33 = np.ascontiguousarray(ws33.transpose(1, 0, 2).reshape(128, 264))
    wg = np.ascontiguousarray(
        Wgt.T.reshape(8, 128, 8, 128).transpose(1, 0, 2, 3).reshape(128, 8192))
    wc = np.ascontiguousarray(Wc.T)
    w1 = np.ascontiguousarray(
        W1[7].reshape(8, 128, 32, 128).transpose(2, 1, 0, 3).reshape(4096, 1024))
    w2 = np.ascontiguousarray(W2[7])
    zo33 = np.zeros((128, 33), np.float32)
    zo33[:, 32] = 1.0
    zsel = np.zeros((128, 33), np.float32)
    zsel[0, 32] = 1.0
    ones = np.ones((128, 128), np.float32)
    b1c = np.ascontiguousarray(b1[7].reshape(32, 128).T)
    b2c = np.ascontiguousarray(b2[7].reshape(8, 128).T)
    rmsw = np.ascontiguousarray(rw.reshape(8, 128).T)
    blam = np.ascontiguousarray(bl.reshape(16, 1))
    zeros_x = np.zeros((128, KC * TW), np.float32)

    def fm(xslice):
        return np.ascontiguousarray(
            xslice.T.reshape(8, 128, TW).transpose(1, 0, 2).reshape(128, KC * TW))

    in_maps = []
    for core in range(NCORES):
        b, h = core // 2, core % 2
        xw = fm(x[b, h * TW:(h + 1) * TW, :])
        xp = fm(x[b, 0:TW, :]) if h == 1 else zeros_x
        xp = np.ascontiguousarray(
            xp.reshape(128, KC, 4, 512).transpose(0, 2, 1, 3)
            .reshape(128, KC * TW))
        in_maps.append({
            "xw": xw, "xp": xp, "wlb": wlb, "wr": wr33, "ws33": ws33,
            "wg": wg, "wc": wc, "w1": w1, "w2": w2, "zo33": zo33, "zsel": zsel,
            "ones": ones, "b1c": b1c, "b2c": b2c, "rmsw": rmsw, "b_lam": blam,
        })
    return in_maps


def kernel(**inputs):
    in_maps = make_in_maps(inputs)
    nc = build_nc()
    res = run_bass_kernel_spmd(nc, in_maps, core_ids=list(range(NCORES)))
    out = np.empty((4, 4096, 1024), np.float32)
    for core in range(NCORES):
        b, h = core // 2, core % 2
        o = np.asarray(res.results[core]["out"])
        o = o.reshape(128, 8, TW).transpose(1, 0, 2).reshape(1024, TW)
        out[b, h * TW:(h + 1) * TW, :] = o.T
    return out
